# version 04
# baseline (speedup 1.0000x reference)
import numpy as np
import concourse.bass as bass
import concourse.mybir as mybir
from concourse.bass_utils import run_bass_kernel_spmd

F32 = mybir.dt.float32
BF16 = mybir.dt.bfloat16
F16 = mybir.dt.float16
I32 = mybir.dt.int32
I8 = mybir.dt.int8
AF = mybir.ActivationFunctionType
ALU = mybir.AluOpType
AX = mybir.AxisListType

NCORES = 8


class Prog:

    COMPUTE = ("pe", "act", "dve", "pool")
    STREAMS = ("pe", "act", "dve", "pool", "qload", "qstore")

    def __init__(self, nc):
        self.nc = nc
        self.ops = []
        self.epoch = 0
        self.cnt = {s: 0 for s in self.STREAMS}
        self.last_w = {}
        self.readers = {}
        self.satisfied = {s: {t: 0 for t in self.STREAMS} for s in self.STREAMS}
        self._names = {}
        self._const_t = None
        self._consts = {}

    def constf(self, value, parts=128):
        if self._const_t is None:
            self._const_t = self.nc.alloc_sbuf_tensor("constpool", [128, 64], F32)
        if value not in self._consts:
            i = len(self._consts)
            self.memset("pool", self._const_t[:, i:i + 1], float(value))
            self._consts[value] = i
        i = self._consts[value]
        return self._const_t[0:parts, i:i + 1]

    def _key(self, ap):
        t = getattr(ap, "tensor", ap)
        return getattr(t, "name", str(id(t)))

    def add(self, stream, emit, reads=(), writes=(), dma=False):
        inc = 16 if dma else 1
        waits = {}
        for ap in list(reads) + list(writes):
            k = self._key(ap)
            lw = self.last_w.get(k)
            if lw is not None:
                s, c = lw
                if not (s == "pe" and stream == "pe"):
                    waits[s] = max(waits.get(s, 0), c)
        for ap in writes:
            k = self._key(ap)
            for s, c in self.readers.get(k, {}).items():
                if not (s == "pe" and stream == "pe"):
                    waits[s] = max(waits.get(s, 0), c)
        sat = self.satisfied[stream]
        waits = {s: c for s, c in waits.items()
                 if c > sat[s] and not (s == stream and c <= self.cnt[stream] and s in ("pe", "act", "dve", "pool") and False)}
        for s, c in waits.items():
            sat[s] = max(sat[s], c)
        self.cnt[stream] += inc
        myc = self.cnt[stream]
        self.ops.append((stream, emit, waits, inc, self.epoch))
        for ap in writes:
            k = self._key(ap)
            self.last_w[k] = (stream, myc)
            self.readers[k] = {}
        for ap in reads:
            k = self._key(ap)
            self.readers.setdefault(k, {})[stream] = myc
        return myc

    def barrier(self):
        snap = dict(self.cnt)
        for s in self.STREAMS:
            waits = {t: c for t, c in snap.items() if c > 0}
            self.ops.append((s, None, waits, 0, self.epoch))
        self.epoch += 1
        self.cnt = {s: 0 for s in self.STREAMS}
        self.satisfied = {s: {t: 0 for t in self.STREAMS} for s in self.STREAMS}
        self.last_w = {}
        self.readers = {}

    def load(self, dst, src, **kw):
        q = "qload"
        try:
            if dst.tensor.dtype != src.tensor.dtype:
                q = "qstore"
        except AttributeError:
            pass
        self.add(q, lambda e: e.dma_start(out=dst, in_=src, **kw),
                 reads=[src], writes=[dst], dma=True)

    def store(self, dst, src, **kw):
        self.add("qstore", lambda e: e.dma_start(out=dst, in_=src, **kw),
                 reads=[src], writes=[dst], dma=True)

    def mm(self, out, lhsT, rhs, start=True, stop=True):
        self.add("pe", lambda e: e.matmul(out, lhsT, rhs, start=start, stop=stop),
                 reads=[lhsT, rhs], writes=[out])

    def trans(self, out, in_, ident):
        self.add("pe", lambda e: e.transpose(out, in_, ident),
                 reads=[in_, ident], writes=[out])

    def act(self, out, in_, func, bias=0.0, scale=1.0, accum_out=None):
        if isinstance(bias, float) and bias != 0.0:
            bias = self.constf(bias, parts=out.shape[0])
        r = [in_]
        if isinstance(bias, bass.AP):
            r.append(bias)
        if isinstance(scale, bass.AP):
            r.append(scale)
        w = [out] + ([accum_out] if accum_out is not None else [])
        self.add("act", lambda e: e.activation(out, in_, func, bias=bias, scale=scale,
                                               accum_out=accum_out),
                 reads=r, writes=w)

    def ew(self, stream, name, *args, reads=(), writes=(), **kw):
        self.add(stream, lambda e: getattr(e, name)(*args, **kw),
                 reads=reads, writes=writes)

    def tt(self, stream, out, a, b, op):
        self.add(stream, lambda e: e.tensor_tensor(out, a, b, op),
                 reads=[a, b], writes=[out])

    def ts(self, stream, out, in0, s1, s2, op0, op1=ALU.bypass, accum_out=None):
        r = [in0]
        if isinstance(s1, bass.AP):
            r.append(s1)
        if isinstance(s2, bass.AP):
            r.append(s2)
        w = [out] + ([accum_out] if accum_out is not None else [])
        self.add(stream, lambda e: e.tensor_scalar(out, in0, s1, s2, op0, op1,
                                                   accum_out=accum_out),
                 reads=r, writes=w)

    def stt(self, stream, out, in0, scalar, in1, op0, op1):
        r = [in0, in1]
        if isinstance(scalar, bass.AP):
            r.append(scalar)
        self.add(stream, lambda e: e.scalar_tensor_tensor(out, in0, scalar, in1, op0, op1),
                 reads=r, writes=[out])

    def copy(self, stream, out, in_):
        self.add(stream, lambda e: e.tensor_copy(out, in_), reads=[in_], writes=[out])

    def memset(self, stream, ap, val):
        self.add(stream, lambda e: e.memset(ap, val), reads=[], writes=[ap])

    def reduce(self, stream, out, in_, op, axis=None):
        axis = axis if axis is not None else AX.X
        self.add(stream, lambda e: e.tensor_reduce(out, in_, axis, op),
                 reads=[in_], writes=[out])

    def build(self):
        nc = self.nc
        import contextlib
        stack = contextlib.ExitStack()
        n_epochs = self.epoch + 1
        sems = {}
        for ep in range(n_epochs):
            for s in self.STREAMS:
                sems[(ep, s)] = stack.enter_context(nc.semaphore(f"sem{ep}_{s}"))
        eng_of = {"pe": "tensor", "act": "scalar", "dve": "vector",
                  "pool": "gpsimd", "qload": "sync", "qstore": "gpsimd"}
        with nc.Block() as blk:
            for eng_name in ("tensor", "scalar", "vector", "gpsimd", "sync"):
                my_streams = [s for s in self.STREAMS if eng_of[s] == eng_name]
                ops = [op for op in self.ops if op[0] in my_streams]
                if not ops:
                    continue

                def body(eng, ops=ops):
                    for (stream, emit, waits, inc, ep) in ops:
                        if emit is None and waits and stream in self.COMPUTE:
                            eng.drain()
                        for s, c in sorted(waits.items()):
                            eng.wait_ge(sems[(ep, s)], c)
                        if emit is not None:
                            inst = emit(eng)
                            inst.then_inc(sems[(ep, stream)], inc)
                getattr(blk, eng_name)(body)
        stack.close()
        return nc



class Ctx:

    def __init__(self, nc, p):
        self.nc = nc
        self.p = p
        self.sb = {}
        self.sb_guard = {}
        self.ps = {}

    _uid = 0

    def S(self, name, shape, dt=F32):
        if name not in self.sb:
            Ctx._uid += 1
            g = self.nc.sbuf_tensor(f"{name}__{Ctx._uid}", shape, dt)
            self.sb[name] = g.__enter__()
            self.sb_guard[name] = g
        return self.sb[name]

    def free_stage_sbuf(self):
        for name in reversed(list(self.sb)):
            if name.startswith("s_c_"):
                continue
            self.sb_guard[name].__exit__(None, None, None)
            del self.sb[name]
            del self.sb_guard[name]

    def P(self, name, shape, dt=F32):
        if name not in self.ps:
            self.ps[name] = self.nc.alloc_psum_tensor(name, shape, dt)
        return self.ps[name]


def make_consts():
    ident = np.eye(128, dtype=np.float32)
    c = {
        "c_ident": ident,
        "c_ident_bf": ident,
        "c_ones_r": np.ones((1, 128), np.float32),
        "c_ones_c": np.ones((128, 1), np.float32),
        "c_mask_su_incl": np.triu(np.ones((128, 128), np.float32)),
        "c_mask_su_strict_neg": -np.triu(np.ones((128, 128), np.float32), 1),
    }
    return c


def declare_consts(ctx, dram_inputs):
    p = ctx.p
    for nm, shape, dt in [("c_ident", [128, 128], F32),
                          ("c_ones_r", [1, 128], F32),
                          ("c_ones_c", [128, 1], F32),
                          ("c_mask_su_incl", [128, 128], F32),
                          ("c_mask_su_strict_neg", [128, 128], F32)]:
        t = ctx.S("s_" + nm, shape, dt)
        p.load(t[:], dram_inputs[nm])
    for nm in ["c_ident", "c_ones_r", "c_ones_c"]:
        t = ctx.S("s_" + nm + "_bf", [128, ctx.sb["s_" + nm].shape[1]] if False else list(ctx.sb["s_" + nm].shape), BF16)
        p.copy("pool", t[:], ctx.sb["s_" + nm][:])


def ln_tile(ctx, out_ap, in_ap, D, pref="ln", eps=1e-5):
    p = ctx.p
    s1 = ctx.S(pref + "_s1", [128, 1])
    s2 = ctx.S(pref + "_s2", [128, 1])
    nm = ctx.S(pref + "_nm", [128, 1])
    e2 = ctx.S(pref + "_e2", [128, 1])
    nv = ctx.S(pref + "_nv", [128, 1])
    l = ctx.S(pref + "_l", [128, 1])
    rstd = ctx.S(pref + "_rstd", [128, 1])
    bb = ctx.S(pref + "_bb", [128, 1])
    scr = ctx.S(pref + "_scr", [128, D])
    p.reduce("dve", s1[:], in_ap, ALU.add)
    p.act(scr[:], in_ap, AF.Square, accum_out=s2[:])
    p.ew("act", "mul", nm[:], s1[:], -1.0 / D, reads=[s1], writes=[nm])
    p.ew("act", "mul", e2[:], s2[:], 1.0 / D, reads=[s2], writes=[e2])
    p.stt("dve", nv[:], nm[:], nm[:, 0:1], e2[:], ALU.mult, ALU.subtract)
    p.act(l[:], nv[:], AF.Ln, scale=-1.0, bias=eps)
    p.act(rstd[:], l[:], AF.Exp, scale=-0.5)
    p.tt("dve", bb[:], nm[:], rstd[:], ALU.mult)
    p.act(out_ap, in_ap, AF.Identity, bias=bb[:], scale=rstd[:])


def transpose128(ctx, dst_ap, src_ap, dt=BF16, pbank="ptr"):
    p = ctx.p
    ident = ctx.sb["s_c_ident_bf" if dt == BF16 else "s_c_ident"]
    ps = ctx.P(pbank, [128, 128], dt)
    p.trans(ps[:], src_ap, ident[:])
    p.copy("dve", dst_ap, ps[:])


def row_bcast(ctx, dst_sb_ap, row_ap, n, dt=F32, pbank="pbc"):
    p = ctx.p
    ones = ctx.sb["s_c_ones_r_bf" if dt == BF16 else "s_c_ones_r"]
    ps = ctx.P(pbank, [128, 512], F32)
    p.mm(ps[:, 0:n], ones[:], row_ap)
    p.copy("dve", dst_sb_ap, ps[:, 0:n])



def stage_l1ffn(ctx, x_in, w_gate, w_up, w_down, s_gate, s_up, s_down,
                x_out, fused_d):
    p = ctx.p
    N, D, I = 4096, 2048, 8192
    A1 = ctx.S("ffn_A1", [128, 32768], BF16)
    A2 = ctx.S("ffn_A2", [128, 16384], BF16)
    I1 = ctx.S("ffn_I1", [128, 16384], I8)
    xt = ctx.S("ffn_xt", [128, 2048])
    xn = ctx.S("ffn_xn", [128, 2048], BF16)
    srow = ctx.S("ffn_srow", [1, 512])
    sbc_g = ctx.S("ffn_sbc_g", [128, 512])
    sbc_u = ctx.S("ffn_sbc_u", [128, 512])
    sdbc = ctx.S("ffn_sdbc", [128, 2048])
    silu_t = ctx.S("ffn_silu", [128, 512])
    su_t = ctx.S("ffn_su", [128, 512])
    fus_t = ctx.S("ffn_fus", [128, 512], BF16)
    y0 = ctx.S("ffn_y0", [128, 2048])
    y1 = ctx.S("ffn_y1", [128, 2048])
    yln = ctx.S("ffn_yln", [128, 2048])
    xres = ctx.S("ffn_xres", [128, 2048])
    pg = ctx.P("pp_g", [128, 512])
    pu = ctx.P("pp_u", [128, 512])
    pd0 = ctx.P("pp_d0", [128, 512])
    pd1 = ctx.P("pp_d1", [128, 512])

    w_gate_r = w_gate.rearrange("(ht p) o -> p ht o", p=128)
    w_up_r = w_up.rearrange("(ht p) o -> p ht o", p=128)
    w_down_r = w_down.rearrange("(ih it p) o -> ih p it o", it=32, p=128)
    s_gate_r = s_gate.rearrange("(a o) -> a o", o=512)
    s_up_r = s_up.rearrange("(a o) -> a o", o=512)
    s_down_r = s_down.rearrange("(a o) -> a o", o=512)

    for half in range(2):
        tok0 = half * 2048
        xnT = A1
        for tt in range(16):
            r0 = tok0 + tt * 128
            p.load(xt[:], x_in[r0:r0 + 128, :])
            ln_tile(ctx, xn[:], xt[:], D)
            for ht in range(16):
                transpose128(ctx, xnT[:, ht * 2048 + tt * 128: ht * 2048 + (tt + 1) * 128],
                             xn[:, ht * 128:(ht + 1) * 128])
        for oc in range(16):
            for wi, (W_r, s_r, wbf, sbc) in enumerate(
                    [(w_gate_r, s_gate_r, A2[:, :8192], sbc_g),
                     (w_up_r, s_up_r, A2[:, 8192:], sbc_u)]):
                wi8v = I1[:, :8192].rearrange("p (ht o) -> p ht o", o=512)
                p.load(wi8v, W_r[:, :, oc * 512:(oc + 1) * 512])
                p.copy("dve", wbf, I1[:, :8192])
                p.load(srow[:], s_r[oc:oc + 1, :])
                row_bcast(ctx, sbc[:], srow[:], 512)
            for tt in range(16):
                for ht in range(16):
                    p.mm(pg[:], xnT[:, ht * 2048 + tt * 128: ht * 2048 + (tt + 1) * 128],
                         A2[:, ht * 512:(ht + 1) * 512],
                         start=(ht == 0), stop=(ht == 15))
                for ht in range(16):
                    p.mm(pu[:], xnT[:, ht * 2048 + tt * 128: ht * 2048 + (tt + 1) * 128],
                         A2[:, 8192 + ht * 512: 8192 + (ht + 1) * 512],
                         start=(ht == 0), stop=(ht == 15))
                p.tt("dve", silu_t[:], pg[:], sbc_g[:], ALU.mult)
                p.act(silu_t[:], silu_t[:], AF.Silu)
                p.tt("dve", su_t[:], pu[:], sbc_u[:], ALU.mult)
                p.tt("dve", fus_t[:], silu_t[:], su_t[:], ALU.mult)
                p.store(fused_d[tok0 + tt * 128: tok0 + (tt + 1) * 128,
                                oc * 512:(oc + 1) * 512], fus_t[:])

    for hc in range(4):
        p.load(srow[:], s_down_r[hc:hc + 1, :])
        row_bcast(ctx, sdbc[:, hc * 512:(hc + 1) * 512], srow[:], 512)
    for g in range(16):
        tok0 = g * 256
        fs = A2
        fusT = A1[:, :16384]
        wdbf = A1[:, 16384:]
        for ts in range(2):
            p.load(fs[:, ts * 8192:(ts + 1) * 8192],
                   fused_d[tok0 + ts * 128: tok0 + (ts + 1) * 128, :])
        for ts in range(2):
            for it in range(64):
                transpose128(ctx, fusT[:, it * 256 + ts * 128: it * 256 + (ts + 1) * 128],
                             fs[:, ts * 8192 + it * 128: ts * 8192 + (it + 1) * 128])
        for hc in range(4):
            for ih in range(2):
                wdi8v = I1[:].rearrange("p (it o) -> p it o", o=512)
                p.load(wdi8v, w_down_r[ih, :, :, hc * 512:(hc + 1) * 512])
                p.copy("dve", wdbf, I1[:])
                for ts, pd in ((0, pd0), (1, pd1)):
                    for it in range(32):
                        itg = ih * 32 + it
                        p.mm(pd[:], fusT[:, itg * 256 + ts * 128: itg * 256 + (ts + 1) * 128],
                             wdbf[:, it * 512:(it + 1) * 512],
                             start=(ih == 0 and it == 0), stop=(ih == 1 and it == 31))
            for ts, pd, y in ((0, pd0, y0), (1, pd1, y1)):
                p.tt("dve", y[:, hc * 512:(hc + 1) * 512], pd[:],
                     sdbc[:, hc * 512:(hc + 1) * 512], ALU.mult)
        for ts, y in ((0, y0), (1, y1)):
            r0 = tok0 + ts * 128
            ln_tile(ctx, yln[:], y[:], D)
            p.load(xres[:], x_in[r0:r0 + 128, :])
            p.tt("dve", yln[:], yln[:], xres[:], ALU.add)
            p.store(x_out[r0:r0 + 128, :], yln[:])



def dequant_fp4(ctx, packed_d, scales_d, WT_d, O, W, in_dim, groups):
    p = ctx.p
    A1 = ctx.S("ffn_A1", [128, 32768], BF16)
    A2 = ctx.S("ffn_A2", [128, 16384], BF16)
    A1i = A1[:].bitcast(I32)
    pk = A1i[:, 0:704]
    nb = A1i[:, 704:1408]
    mm_ = A1i[:, 1408:2112]
    sg = A1i[:, 2112:2816]
    t1 = A1i[:, 2816:3520]
    t2 = A1i[:, 3520:4224]
    dqi = A1i[:, 4224:9856]
    dqs = A2[:, 0:5632]
    sc16 = A2[0:44, 5632:5760].bitcast(F16)
    sc32 = A2[0:44, 5760:6016].bitcast(F32)
    scT = A2[:, 6016:6104].bitcast(F32)
    wtile = A2[:, 6104:6232]
    ptf = ctx.P("ptf", [128, 128], F32)
    ident = ctx.sb["s_c_ident"]
    for ot in range(O // 128):
        p.load(pk[:, :W], packed_d[ot * 128:(ot + 1) * 128, :])
        p.load(sc16[:groups, :], scales_d[:, ot * 128:(ot + 1) * 128])
        p.copy("pool", sc32[:groups, :], sc16[:groups, :])
        p.trans(ptf[:, :groups], sc32[:groups, :], ident[:groups, :groups])
        p.copy("dve", scT[:, :groups], ptf[:, :groups])
        for j in range(8):
            p.ts("dve", nb[:, :W], pk[:, :W], 4 * j, 15,
                 ALU.logical_shift_right, ALU.bitwise_and)
            p.ts("dve", mm_[:, :W], nb[:, :W], 7, None, ALU.bitwise_and)
            p.ts("dve", sg[:, :W], nb[:, :W], 3, 31,
                 ALU.logical_shift_right, ALU.logical_shift_left)
            p.ts("dve", t1[:, :W], mm_[:, :W], 0x400000, 0x3F800000 - 2 * 0x400000,
                 ALU.mult, ALU.add)
            p.stt("dve", t2[:, :W], mm_[:, :W], 0x3F000000, t1[:, :W],
                  ALU.mult, ALU.min)
            dqv = dqi.bitcast(I32) if False else dqi
            p.tt("dve", dqv[:, j:in_dim:8], t2[:, :W], sg[:, :W], ALU.bitwise_or)
        for g in range(groups):
            p.ts("dve", dqs[:, g * 128:(g + 1) * 128],
                 dqi[:, g * 128:(g + 1) * 128].bitcast(F32),
                 scT[:, g:g + 1], None, ALU.mult)
        for it in range(in_dim // 128):
            transpose128(ctx, wtile[:, 0:128], dqs[:, it * 128:(it + 1) * 128])
            p.store(WT_d[it * 128:(it + 1) * 128, ot * 128:(ot + 1) * 128], wtile[:, 0:128])


def stage_l0ffn(ctx, x_in, gup_d, gus_d, dnp_d, dns_d, x_out,
                WguT_d, WdT_d, fused_d):
    p = ctx.p
    D, I = 2048, 5632
    dequant_fp4(ctx, gup_d, gus_d, WguT_d, 11264, 256, 2048, 16)
    dequant_fp4(ctx, dnp_d, dns_d, WdT_d, 2048, 704, 5632, 44)

    A1 = ctx.S("ffn_A1", [128, 32768], BF16)
    A2 = ctx.S("ffn_A2", [128, 16384], BF16)
    xt = ctx.S("ffn_xt", [128, 2048])
    xn = ctx.S("ffn_xn", [128, 2048], BF16)
    silu_t = ctx.S("ffn_silu", [128, 512])
    fus_t = ctx.S("ffn_fus", [128, 512], BF16)
    y0 = ctx.S("ffn_y0", [128, 2048])
    y1 = ctx.S("ffn_y1", [128, 2048])
    yln = ctx.S("ffn_yln", [128, 2048])
    xres = ctx.S("ffn_xres", [128, 2048])
    rs1 = ctx.S("rms_1", [128, 1])
    rs2 = ctx.S("rms_2", [128, 1])
    rscr = ctx.S("rms_scr", [128, 2048])
    pg = ctx.P("pp_g", [128, 512])
    pu = ctx.P("pp_u", [128, 512])
    pd0 = ctx.P("pp_d0", [128, 512])
    pd1 = ctx.P("pp_d1", [128, 512])

    WguT_r = WguT_d.rearrange("(ht p) o -> p ht o", p=128)
    WdT_r = WdT_d.rearrange("(ih it p) o -> ih p it o", it=22, p=128)

    for half in range(2):
        tok0 = half * 2048
        xnT = A1
        for tt in range(16):
            r0 = tok0 + tt * 128
            p.load(xt[:], x_in[r0:r0 + 128, :])
            ln_tile(ctx, xn[:], xt[:], D)
            for ht in range(16):
                transpose128(ctx, xnT[:, ht * 2048 + tt * 128: ht * 2048 + (tt + 1) * 128],
                             xn[:, ht * 128:(ht + 1) * 128])
        for oc in range(11):
            wg = A2[:, :8192]
            wu = A2[:, 8192:]
            p.load(wg.rearrange("p (ht o) -> p ht o", o=512),
                   WguT_r[:, :, oc * 512:(oc + 1) * 512])
            p.load(wu.rearrange("p (ht o) -> p ht o", o=512),
                   WguT_r[:, :, 5632 + oc * 512: 5632 + (oc + 1) * 512])
            for tt in range(16):
                for ht in range(16):
                    p.mm(pg[:], xnT[:, ht * 2048 + tt * 128: ht * 2048 + (tt + 1) * 128],
                         wg[:, ht * 512:(ht + 1) * 512],
                         start=(ht == 0), stop=(ht == 15))
                for ht in range(16):
                    p.mm(pu[:], xnT[:, ht * 2048 + tt * 128: ht * 2048 + (tt + 1) * 128],
                         wu[:, ht * 512:(ht + 1) * 512],
                         start=(ht == 0), stop=(ht == 15))
                p.act(silu_t[:], pg[:], AF.Silu)
                p.tt("dve", fus_t[:], silu_t[:], pu[:], ALU.mult)
                p.store(fused_d[tok0 + tt * 128: tok0 + (tt + 1) * 128,
                                oc * 512:(oc + 1) * 512], fus_t[:])

    for g in range(16):
        tok0 = g * 256
        fs = A2
        fusT = A1[:, :11264]
        wdbf = A1[:, 11264: 11264 + 11264]
        for ts_ in range(2):
            p.load(fs[:, ts_ * 5632:(ts_ + 1) * 5632],
                   fused_d[tok0 + ts_ * 128: tok0 + (ts_ + 1) * 128, :])
        for ts_ in range(2):
            for it in range(44):
                transpose128(ctx, fusT[:, it * 256 + ts_ * 128: it * 256 + (ts_ + 1) * 128],
                             fs[:, ts_ * 5632 + it * 128: ts_ * 5632 + (it + 1) * 128])
        for hc in range(4):
            for ih in range(2):
                p.load(wdbf.rearrange("p (it o) -> p it o", o=512),
                       WdT_r[ih, :, :, hc * 512:(hc + 1) * 512])
                for ts_, pd in ((0, pd0), (1, pd1)):
                    for it in range(22):
                        itg = ih * 22 + it
                        p.mm(pd[:], fusT[:, itg * 256 + ts_ * 128: itg * 256 + (ts_ + 1) * 128],
                             wdbf[:, it * 512:(it + 1) * 512],
                             start=(ih == 0 and it == 0), stop=(ih == 1 and it == 21))
            for ts_, pd, y in ((0, pd0, y0), (1, pd1, y1)):
                p.copy("dve", y[:, hc * 512:(hc + 1) * 512], pd[:])
        for ts_, y in ((0, y0), (1, y1)):
            r0 = tok0 + ts_ * 128
            p.act(rscr[:], y[:], AF.Square, accum_out=rs1[:])
            p.act(rs2[:], rs1[:], AF.Ln, scale=1.0 / 2048, bias=1e-6)
            p.act(rs1[:], rs2[:], AF.Exp, scale=-0.5)
            p.ts("dve", y[:], y[:], rs1[:, 0:1], None, ALU.mult)
            ln_tile(ctx, yln[:], y[:], D)
            p.load(xres[:], x_in[r0:r0 + 128, :])
            p.tt("dve", yln[:], yln[:], xres[:], ALU.add)
            p.store(x_out[r0:r0 + 128, :], yln[:])



def _new_prog():
    nc = bass.Bass("TRN2", target_bir_lowering=False)
    p = Prog(nc)
    ctx = Ctx(nc, p)
    dins = {}
    for nm, arr in make_consts().items():
        if nm.endswith("_bf"):
            continue
        dins[nm] = nc.dram_tensor(nm, list(arr.shape), F32, kind="ExternalInput").ap()
    declare_consts(ctx, dins)
    p.constf(1e-5)
    p.barrier()
    return nc, p, ctx


def build_s4_program():
    nc, p, ctx = _new_prog()
    N, D, I = 4096, 2048, 8192
    x_in = nc.dram_tensor("x_in", [N, D], F32, kind="ExternalInput").ap()
    wg = nc.dram_tensor("w_gate", [D, I], I8, kind="ExternalInput").ap()
    wu = nc.dram_tensor("w_up", [D, I], I8, kind="ExternalInput").ap()
    wd = nc.dram_tensor("w_down", [I, D], I8, kind="ExternalInput").ap()
    sg = nc.dram_tensor("s_gate", [I], F32, kind="ExternalInput").ap()
    su = nc.dram_tensor("s_up", [I], F32, kind="ExternalInput").ap()
    sd = nc.dram_tensor("s_down", [D], F32, kind="ExternalInput").ap()
    x_out = nc.dram_tensor("x_out", [N, D], F32, kind="ExternalOutput").ap()
    fused_d = nc.dram_tensor("fused_d", [N, I], BF16, kind="Internal").ap()
    stage_l1ffn(ctx, x_in, wg, wu, wd, sg, su, sd, x_out, fused_d)
    p.barrier()
    return p.build()


def build_s2_program():
    nc, p, ctx = _new_prog()
    N, D, I = 4096, 2048, 5632
    x_in = nc.dram_tensor("x_in", [N, D], F32, kind="ExternalInput").ap()
    gup = nc.dram_tensor("gup", [11264, 256], I32, kind="ExternalInput").ap()
    gus = nc.dram_tensor("gus", [16, 11264], F16, kind="ExternalInput").ap()
    dnp = nc.dram_tensor("dnp", [2048, 704], I32, kind="ExternalInput").ap()
    dns = nc.dram_tensor("dns", [44, 2048], F16, kind="ExternalInput").ap()
    x_out = nc.dram_tensor("x_out", [N, D], F32, kind="ExternalOutput").ap()
    WguT = nc.dram_tensor("WguT", [2048, 11264], BF16, kind="Internal").ap()
    WdT = nc.dram_tensor("WdT", [5632, 2048], BF16, kind="Internal").ap()
    fused_d = nc.dram_tensor("fused_d", [N, I], BF16, kind="Internal").ap()
    stage_l0ffn(ctx, x_in, gup, gus, dnp, dns, x_out, WguT, WdT, fused_d)
    p.barrier()
    return p.build()



def transpose_w(ctx, W_d, WT_d, wpref="wtr"):
    p = ctx.p
    wrow = ctx.S(wpref + "_row", [128, 2048])
    wbf = ctx.S(wpref + "_bf", [128, 2048], BF16)
    wt = ctx.S(wpref + "_t", [128, 128], BF16)
    for rt in range(16):
        p.load(wrow[:], W_d[rt * 128:(rt + 1) * 128, :])
        p.copy("pool", wbf[:], wrow[:])
        for ct in range(16):
            transpose128(ctx, wt[:], wbf[:, ct * 128:(ct + 1) * 128])
            p.store(WT_d[ct * 128:(ct + 1) * 128, rt * 128:(rt + 1) * 128], wt[:])


def stage_l1attn(ctx, x_in, Wq_d, Wk_d, Wv_d, Wo_d, cosT_d, sinT_d, x_out,
                 WqT_d, WkT_d, WvT_d, WoT_d, attnT_d, yat_d):
    p = ctx.p
    D, T, NH = 2048, 2048, 16
    for W_d, WT_d in [(Wq_d, WqT_d), (Wk_d, WkT_d), (Wv_d, WvT_d), (Wo_d, WoT_d)]:
        transpose_w(ctx, W_d, WT_d)

    xnT = ctx.S("at_xnT", [128, 16 * 2048], BF16)
    xt = ctx.S("at_xt", [128, 2048])
    xn = ctx.S("at_xn", [128, 2048], BF16)
    cosT = ctx.S("at_cos", [64, 2048], BF16)
    sinT = ctx.S("at_sin", [64, 2048], BF16)
    p4all = ctx.S("at_p4all", [128, 8192], BF16)
    p.load(cosT[:], cosT_d)
    p.load(sinT[:], sinT_d)
    wql = ctx.S("at_wql", [128, 16 * 512], BF16)
    wkl = wql
    wvl = wql
    qTr = ctx.S("at_qT", [128, 4 * 2048], BF16)
    kTr = ctx.S("at_kT", [128, 4 * 2048], BF16)
    v_sb = ctx.S("at_v", [128, 16 * 512], BF16)
    r1 = ctx.S("at_r1", [64, 512])
    r2 = ctx.S("at_r2", [64, 512])
    p2 = ctx.S("at_p2", [128, 512])
    rden = ctx.S("at_rden", [1, 512])
    rdbc = ctx.S("at_rdbc", [128, 512])
    oT_f = ctx.S("at_oTf", [128, 512], BF16)
    wol = wvl
    y_t = p4all[:, 0:4096].bitcast(F32)
    yln = ctx.S("at_yln", [128, 2048])
    xres = xt
    pqk = ctx.P("pp_g", [128, 512])
    poT = ctx.P("pp_u", [128, 512])
    pden = ctx.P("pp_d0", [128, 512])
    pbc2 = ctx.P("pp_d1", [128, 512])
    mask = ctx.sb["s_c_mask_su_incl"]
    onesc_bf = ctx.sb["s_c_ones_c_bf"]

    WqT_r = WqT_d.rearrange("(ht p) o -> p ht o", p=128)
    WkT_r = WkT_d.rearrange("(ht p) o -> p ht o", p=128)
    WvT_r = WvT_d.rearrange("(ht p) o -> p ht o", p=128)
    WoT_r = WoT_d.rearrange("(ht p) o -> p ht o", p=128)

    for b in range(2):
        tok0 = b * 2048
        for tt in range(16):
            r0 = tok0 + tt * 128
            p.load(xt[:], x_in[r0:r0 + 128, :])
            ln_tile(ctx, xn[:], xt[:], D)
            for ht in range(16):
                transpose128(ctx, xnT[:, ht * 2048 + tt * 128: ht * 2048 + (tt + 1) * 128],
                             xn[:, ht * 128:(ht + 1) * 128])
        for grp in range(4):
            c0 = grp * 512
            for W_r, dst in ((WqT_r, qTr), (WkT_r, kTr)):
                p.load(wql[:].rearrange("p (ht o) -> p ht o", o=512), W_r[:, :, c0:c0 + 512])
                for hh in range(4):
                    for tc in range(4):
                        for ht in range(16):
                            p.mm(pqk[:], wql[:, ht * 512 + hh * 128: ht * 512 + (hh + 1) * 128],
                                 xnT[:, ht * 2048 + tc * 512: ht * 2048 + (tc + 1) * 512],
                                 start=(ht == 0), stop=(ht == 15))
                        cs = cosT[:, tc * 512:(tc + 1) * 512]
                        sn = sinT[:, tc * 512:(tc + 1) * 512]
                        d0 = hh * 2048 + tc * 512
                        p.tt("dve", r1[:], pqk[0:64, :], cs, ALU.mult)
                        p.tt("dve", r2[:], pqk[64:128, :], sn, ALU.mult)
                        p.tt("dve", dst[0:64, d0:d0 + 512], r1[:], r2[:], ALU.subtract)
                        p.tt("dve", r1[:], pqk[0:64, :], sn, ALU.mult)
                        p.tt("dve", r2[:], pqk[64:128, :], cs, ALU.mult)
                        p.tt("dve", dst[64:128, d0:d0 + 512], r1[:], r2[:], ALU.add)
            p.load(wvl[:].rearrange("p (ht o) -> p ht o", o=512), WvT_r[:, :, c0:c0 + 512])
            for tt in range(16):
                for ht in range(16):
                    p.mm(pqk[:], xnT[:, ht * 2048 + tt * 128: ht * 2048 + (tt + 1) * 128],
                         wvl[:, ht * 512:(ht + 1) * 512],
                         start=(ht == 0), stop=(ht == 15))
                p.copy("dve", v_sb[:, tt * 512:(tt + 1) * 512], pqk[:])
            for hh in range(4):
                head = grp * 4 + hh
                for tc in range(4):
                    nsc = 4 * tc + 4
                    for sc in range(nsc):
                        p4 = p4all[:, sc * 512:(sc + 1) * 512]
                        p.mm(pqk[:], kTr[:, hh * 2048 + sc * 128: hh * 2048 + (sc + 1) * 128],
                             qTr[:, hh * 2048 + tc * 512: hh * 2048 + (tc + 1) * 512])
                        j = sc - 4 * tc
                        lo = max(0, j * 128)
                        p.act(p2[:, lo:512], pqk[:, lo:512], AF.Square)
                        p.act(p4[:, lo:512], p2[:, lo:512], AF.Square)
                        if lo > 0:
                            p.memset("dve", p4[:, 0:lo], 0.0)
                        if j >= 0:
                            p.tt("dve", p4[:, lo:lo + 128], p4[:, lo:lo + 128],
                                 mask[:], ALU.mult)
                    for sc in range(nsc):
                        p.mm(poT[:], v_sb[:, sc * 512 + hh * 128: sc * 512 + (hh + 1) * 128],
                             p4all[:, sc * 512:(sc + 1) * 512],
                             start=(sc == 0), stop=(sc == nsc - 1))
                    for sc in range(nsc):
                        p.mm(pden[0:1, :], onesc_bf[:], p4all[:, sc * 512:(sc + 1) * 512],
                             start=(sc == 0), stop=(sc == nsc - 1))
                    p.ts("dve", rden[:], pden[0:1, :], 1e-6, None, ALU.max)
                    p.ew("dve", "reciprocal", rden[:], rden[:],
                         reads=[rden], writes=[rden])
                    p.mm(pbc2[:], ctx.sb["s_c_ones_r"][:], rden[:])
                    p.copy("dve", rdbc[:], pbc2[:])
                    p.tt("dve", oT_f[:], poT[:], rdbc[:], ALU.mult)
                    p.store(attnT_d[b, head * 128:(head + 1) * 128,
                                    tc * 512:(tc + 1) * 512], oT_f[:])
        for hc in range(4):
            p.load(wol[:].rearrange("p (ht o) -> p ht o", o=512),
                   WoT_r[:, :, hc * 512:(hc + 1) * 512])
            for tchunk in range(16):
                for dt in range(16):
                    p.load(v_sb[:, dt * 128:(dt + 1) * 128],
                           attnT_d[b, dt * 128:(dt + 1) * 128,
                                   tchunk * 128:(tchunk + 1) * 128])
                for dt in range(16):
                    p.mm(pqk[:], v_sb[:, dt * 128:(dt + 1) * 128],
                         wol[:, dt * 512:(dt + 1) * 512],
                         start=(dt == 0), stop=(dt == 15))
                p.copy("dve", p2[:], pqk[:])
                p.store(yat_d[b, tchunk * 128:(tchunk + 1) * 128,
                              hc * 512:(hc + 1) * 512], p2[:])
        for tt in range(16):
            r0 = tok0 + tt * 128
            p.load(y_t[:, 0:2048], yat_d[b, tt * 128:(tt + 1) * 128, :])
            ln_tile(ctx, yln[:], y_t[:, 0:2048], D)
            p.load(xres[:], x_in[r0:r0 + 128, :])
            p.tt("dve", yln[:], yln[:], xres[:], ALU.add)
            p.store(x_out[r0:r0 + 128, :], yln[:])


def make_s3_consts():
    T, half = 2048, 64
    theta = 10000.0 ** (-np.arange(half, dtype=np.float64) * 2.0 / 128.0)
    ang = np.arange(T, dtype=np.float64)[None, :] * theta[:, None]
    return {"c_cosT": np.cos(ang).astype(np.float32),
            "c_sinT": np.sin(ang).astype(np.float32)}


def build_s3_program():
    nc, p, ctx = _new_prog()
    N, D = 4096, 2048
    x_in = nc.dram_tensor("x_in", [N, D], F32, kind="ExternalInput").ap()
    Wq = nc.dram_tensor("Wq", [D, D], F32, kind="ExternalInput").ap()
    Wk = nc.dram_tensor("Wk", [D, D], F32, kind="ExternalInput").ap()
    Wv = nc.dram_tensor("Wv", [D, D], F32, kind="ExternalInput").ap()
    Wo = nc.dram_tensor("Wo", [D, D], F32, kind="ExternalInput").ap()
    cosT = nc.dram_tensor("c_cosT", [64, 2048], F32, kind="ExternalInput").ap()
    sinT = nc.dram_tensor("c_sinT", [64, 2048], F32, kind="ExternalInput").ap()
    x_out = nc.dram_tensor("x_out", [N, D], F32, kind="ExternalOutput").ap()
    WqT = nc.dram_tensor("WqT", [D, D], BF16, kind="Internal").ap()
    WkT = nc.dram_tensor("WkT", [D, D], BF16, kind="Internal").ap()
    WvT = nc.dram_tensor("WvT", [D, D], BF16, kind="Internal").ap()
    WoT = nc.dram_tensor("WoT", [D, D], BF16, kind="Internal").ap()
    attnT = nc.dram_tensor("attnT_d", [2, D, 2048], BF16, kind="Internal").ap()
    yat = nc.dram_tensor("yat_d", [2, 2048, 2048], F32, kind="Internal").ap()
    stage_l1attn(ctx, x_in, Wq, Wk, Wv, Wo, cosT, sinT, x_out,
                 WqT, WkT, WvT, WoT, attnT, yat)
    p.barrier()
    return p.build()



def stage_s1a(ctx, x_in, Wqkvz_d, Wba_d, Alog_d, dtb_d,
              mixedT_d, zT_d, G_d, lnb_d):
    p = ctx.p
    D = 2048
    xnT = ctx.S("da_xnT", [128, 16 * 2048], BF16)
    xt = ctx.S("da_xt", [128, 2048])
    xn = ctx.S("da_xn", [128, 2048], BF16)
    wcol = ctx.S("da_wcol", [128, 16 * 128])
    wcol_bf = ctx.S("da_wcolb", [128, 16 * 128], BF16)
    wba = ctx.S("da_wba", [128, 16 * 64])
    wba_bf = ctx.S("da_wbab", [128, 16 * 64], BF16)
    mx_t = ctx.S("da_mxt", [128, 512], BF16)
    baT = ctx.S("da_baT", [64, 2048])
    b_sb = ctx.S("da_b", [32, 2048])
    a_sb = ctx.S("da_a", [32, 2048])
    g_sb = ctx.S("da_g", [32, 2048])
    lnb = ctx.S("da_lnb", [32, 2048])
    G_sb = ctx.S("da_G", [32, 2048])
    z32 = ctx.S("da_z32", [32, 128])
    sp1 = ctx.S("da_sp1", [32, 2048])
    sp2 = ctx.S("da_sp2", [32, 2048])
    sp3 = ctx.S("da_sp3", [32, 2048])
    alog = ctx.S("da_alog", [32, 1])
    dtb = ctx.S("da_dtb", [32, 1])
    eal = ctx.S("da_eal", [32, 1])
    ba_scr = ctx.nc.dram_tensor("ba_scr", [64, 2048], F32, kind="Internal").ap()
    pq = ctx.P("pp_g", [128, 512])
    Wq_r = Wqkvz_d
    p.load(alog[:], Alog_d.rearrange("(a b) -> a b", b=1))
    p.load(dtb[:], dtb_d.rearrange("(a b) -> a b", b=1))
    p.act(eal[:], alog[:], AF.Exp)
    p.memset("dve", z32[:], 0.0)

    for b in range(2):
        tok0 = b * 2048
        for tt in range(16):
            r0 = tok0 + tt * 128
            p.load(xt[:], x_in[r0:r0 + 128, :])
            ln_tile(ctx, xn[:], xt[:], D)
            for ht in range(16):
                transpose128(ctx, xnT[:, ht * 2048 + tt * 128: ht * 2048 + (tt + 1) * 128],
                             xn[:, ht * 128:(ht + 1) * 128])
        for ct in range(96):
            p.load(wcol[:].rearrange("p (ht c) -> p ht c", c=128),
                   Wq_r.rearrange("(ht p) c -> p ht c", p=128)[:, :, ct * 128:(ct + 1) * 128])
            p.copy("pool", wcol_bf[:], wcol[:])
            g6, j = ct // 6, ct % 6
            if j == 0:
                dst, row = mixedT_d, g6 * 128
            elif j == 1:
                dst, row = mixedT_d, 2048 + g6 * 128
            elif j in (2, 3):
                dst, row = mixedT_d, 4096 + (2 * g6 + j - 2) * 128
            else:
                dst, row = zT_d, (2 * g6 + j - 4) * 128
            for tsc in range(4):
                for ht in range(16):
                    p.mm(pq[:], wcol_bf[:, ht * 128:(ht + 1) * 128],
                         xnT[:, ht * 2048 + tsc * 512: ht * 2048 + (tsc + 1) * 512],
                         start=(ht == 0), stop=(ht == 15))
                p.copy("dve", mx_t[:], pq[:])
                p.store(dst[b, row:row + 128, tsc * 512:(tsc + 1) * 512], mx_t[:])
        p.load(wba[:].rearrange("p (ht c) -> p ht c", c=64),
               Wba_d.rearrange("(ht p) c -> p ht c", p=128))
        p.copy("pool", wba_bf[:], wba[:])
        for tsc in range(4):
            for ht in range(16):
                p.mm(pq[0:64, :], wba_bf[:, ht * 64:(ht + 1) * 64],
                     xnT[:, ht * 2048 + tsc * 512: ht * 2048 + (tsc + 1) * 512],
                     start=(ht == 0), stop=(ht == 15))
            p.copy("dve", baT[:, tsc * 512:(tsc + 1) * 512], pq[0:64, :])
        p.store(ba_scr, baT[:])
        for hk in range(16):
            p.load(b_sb[2 * hk:2 * hk + 2, :], ba_scr[4 * hk:4 * hk + 2, :])
            p.load(a_sb[2 * hk:2 * hk + 2, :], ba_scr[4 * hk + 2:4 * hk + 4, :])
        p.ts("dve", sp1[:], a_sb[:], dtb[:, 0:1], None, ALU.add)
        p.act(sp2[:], sp1[:], AF.Abs)
        p.act(sp2[:], sp2[:], AF.Exp, scale=-1.0)
        p.act(sp2[:], sp2[:], AF.Ln, bias=1.0)
        p.act(sp3[:], sp1[:], AF.Relu)
        p.tt("dve", sp1[:], sp2[:], sp3[:], ALU.add)
        p.ts("dve", g_sb[:], sp1[:], eal[:, 0:1], -1.0, ALU.mult, ALU.mult)
        p.act(sp2[:], b_sb[:], AF.Abs)
        p.act(sp2[:], sp2[:], AF.Exp, scale=-1.0)
        p.act(sp2[:], sp2[:], AF.Ln, bias=1.0)
        p.act(sp3[:], b_sb[:], AF.Relu, scale=-1.0)
        p.stt("dve", lnb[:], sp3[:], -1.0, sp2[:], ALU.mult, ALU.subtract)
        for c in range(16):
            p.add("dve", lambda e, c=c: e.tensor_tensor_scan(
                G_sb[:, c * 128:(c + 1) * 128], g_sb[:, c * 128:(c + 1) * 128],
                z32[:], 0.0, ALU.add, ALU.add),
                reads=[g_sb, z32], writes=[G_sb])
        p.store(G_d[b], G_sb[:])
        p.store(lnb_d[b], lnb[:])


def stage_s1b(ctx, x_in, mixedT_d, zT_d, G_d, lnb_d, convw_d, normw_d,
              Wout_d, x_out, gatedT_d, y_d):
    p = ctx.p
    C = 128
    mx = ctx.S("db_mx", [128, 64 * 131], BF16)
    AR = ctx.S("db_AR", [128, 8192])
    cv = ctx.S("db_cv", [128, 8192], BF16)
    Ssb = ctx.S("db_S", [128, 32 * 128])
    khf = ctx.S("db_khf", [128, 32 * 128])
    khb = ctx.S("db_khb", [128, 32 * 128], BF16)
    OT = ctx.S("db_OT", [128, 32 * 128])
    zld = None
    cw = ctx.S("db_cw", [128, 64 * 4], BF16)
    cwf = ctx.S("db_cwf", [128, 64 * 4])
    Gsb = ctx.S("db_G", [32, 2048])
    lnbs = ctx.S("db_lnb", [32, 2048])
    GLB = ctx.S("db_GLB", [32, 2048])
    glbrow = ctx.S("db_glbrow", [1, 128])
    grow = ctx.S("db_grow", [1, 128])
    glb_scr = ctx.nc.dram_tensor("glb_scr", [2, 32, 2048], F32, kind="Internal").ap()
    GT = ctx.S("db_GT", [128, 32])
    lnbT = ctx.S("db_lnbT", [128, 32])
    GLBT = ctx.S("db_GLBT", [128, 32])
    EG = ctx.S("db_EG", [128, 32])
    EGLBn = ctx.S("db_EGLBn", [128, 32])
    bT = ctx.S("db_bT", [128, 32])
    egl_c = ctx.S("db_eglc", [32, 1])
    egl_r = ctx.S("db_eglr", [1, 32])
    scr = ctx.S("db_scr", [128, 2048])
    nr = ctx.S("db_nr", [1, 512])
    rn_sl = ctx.S("db_rn", [1, 512])
    x1 = ctx.S("db_x1", [128, 128])
    xe = ctx.S("db_xe", [128, 128])
    ntf = ctx.S("db_ntf", [128, 128])
    NT0 = ctx.S("db_NT", [128, 128], BF16)
    N0 = ctx.S("db_N0", [128, 128], BF16)
    Xb = ctx.S("db_X", [128, 6 * 128], BF16)
    Pb = ctx.S("db_P", [128, 7 * 128], BF16)
    Vtm = ctx.S("db_Vtm", [128, 128])
    bV = ctx.S("db_bV", [128, 128])
    tmpf = ctx.S("db_tmpf", [128, 128])
    tmpb = ctx.S("db_tmpb", [128, 128], BF16)
    O1 = ctx.S("db_O1", [128, 128])
    Osb = ctx.S("db_Osb", [128, 128])
    Ut = ctx.S("db_Ut", [128, 128])
    khtm = ctx.S("db_khtm", [128, 128])
    dqm = ctx.S("db_dqm", [128, 128])
    sfs = ctx.S("db_sfs", [128, 1])
    nws = ctx.S("db_nw", [128, 1])
    ssqr = ctx.S("db_ssqr", [1, 512])
    rr = ctx.S("db_rr", [1, 512])
    y_t = ctx.S("db_y", [128, 2048])
    pA = ctx.P("pp_g", [128, 512])
    pB = ctx.P("pp_u", [128, 512])
    pC = ctx.P("pp_d0", [128, 512])
    pD = ctx.P("pp_d1", [128, 512])
    ptr = ctx.P("ptr", [128, 128], BF16)
    ptf = ctx.P("ptf", [128, 128], F32)
    ident = ctx.sb["s_c_ident"]
    identb = ctx.sb["s_c_ident_bf"]
    onesr = ctx.sb["s_c_ones_r"]
    onesc = ctx.sb["s_c_ones_c"]
    m_su_neg = ctx.sb["s_c_mask_su_strict_neg"]
    m_su_inc = ctx.sb["s_c_mask_su_incl"]

    p.load(nws[:], normw_d.rearrange("(a b) -> a b", b=1))
    p.load(cw[:].rearrange("p (ct k) -> p ct k", k=4),
           convw_d.rearrange("(ct p) k -> p ct k", p=128))
    LOG_DKI = float(np.log(1.0 / np.sqrt(128.0)))

    for b in range(2):
        tok0 = b * 2048
        p.load(Gsb[:], G_d[b])
        p.load(lnbs[:], lnb_d[b])
        p.tt("dve", GLB[:], Gsb[:], lnbs[:], ALU.add)
        p.store(glb_scr[b], GLB[:])
        p.memset("dve", Ssb[:], 0.0)
        for c in range(16):
            c0 = c * 128
            mxv = mx[:].rearrange("p (ct t) -> p ct t", t=131)
            if c == 0:
                p.memset("dve", mxv[:, :, 0:3], 0.0)
                p.load(mxv[:, :, 3:131],
                       mixedT_d[b].rearrange("(ct p) t -> p ct t", p=128)[:, :, 0:128])
            else:
                p.load(mxv[:, :, 0:131],
                       mixedT_d[b].rearrange("(ct p) t -> p ct t", p=128)[:, :, c0 - 3:c0 + 128])
            ARv = AR[:].rearrange("p (ct t) -> p ct t", t=128)
            cvv = cv[:].rearrange("p (ct t) -> p ct t", t=128)
            cwv = cw[:].rearrange("p (ct k) -> p ct k", k=4)
            for k in range(4):
                wk = cwv[:, :, k:k + 1].broadcast_to([128, 64, 128])
                if k == 0:
                    p.tt("dve", ARv[:], mxv[:, :, 0:128], wk, ALU.mult)
                else:
                    p.tt("dve", cvv[:], mxv[:, :, k:k + 128], wk, ALU.mult)
                    p.tt("dve", ARv[:], ARv[:], cvv[:], ALU.add)
            p.act(cv[:], AR[:], AF.Silu)
            p.trans(ptf[:, 0:32], Gsb[:, c0:c0 + 128], ident[0:32, 0:32])
            p.copy("dve", GT[:], ptf[:, 0:32])
            p.trans(ptf[:, 0:32], lnbs[:, c0:c0 + 128], ident[0:32, 0:32])
            p.copy("dve", lnbT[:], ptf[:, 0:32])
            p.tt("dve", GLBT[:], GT[:], lnbT[:], ALU.add)
            p.act(EG[:], GT[:], AF.Exp)
            p.act(EGLBn[:], GLBT[:], AF.Exp)
            p.ts("dve", EGLBn[:], EGLBn[:], -1.0, None, ALU.mult)
            p.act(bT[:], lnbT[:], AF.Exp)
            p.act(egl_c[:], Gsb[:, c0 + 127:c0 + 128], AF.Exp)
            p.trans(ptf[0:1, 0:32], egl_c[:], ident[0:32, 0:32])
            p.copy("dve", egl_r[:], ptf[0:1, 0:32])
            for base, bias in ((0, LOG_DKI), (2048, 0.0)):
                p.act(scr[:], cv[:, base:base + 2048], AF.Square)
                for s4 in range(4):
                    p.mm(pC[0:1, :], onesc[:], scr[:, s4 * 512:(s4 + 1) * 512])
                    p.copy("dve", nr[:], pC[0:1, :])
                    p.act(nr[:], nr[:], AF.Ln, bias=1e-6)
                    p.act(rn_sl[:], nr[:], AF.Exp, scale=-0.5, bias=bias)
                    p.mm(pC[:], onesr[:], rn_sl[:])
                    p.tt("dve", khf[:, base + s4 * 512: base + (s4 + 1) * 512],
                         cv[:, base + s4 * 512: base + (s4 + 1) * 512], pC[:], ALU.mult)
            p.copy("pool", khb[:], khf[:])
            for h in range(32):
                kv = h // 2
                qf = khf[:, kv * 128:(kv + 1) * 128]
                qb = khb[:, kv * 128:(kv + 1) * 128]
                kf = khf[:, 2048 + kv * 128: 2048 + (kv + 1) * 128]
                kb = khb[:, 2048 + kv * 128: 2048 + (kv + 1) * 128]
                vt = cv[:, 4096 + h * 128: 4096 + (h + 1) * 128]
                Sh = Ssb[:, h * 128:(h + 1) * 128]
                p.mm(pA[:, 0:128], kb, kb)
                p.load(glbrow[:], glb_scr[b, h:h + 1, c0:c0 + 128])
                p.mm(pA[:, 128:256], onesr[:], glbrow[:])
                p.ts("dve", x1[:], pA[:, 128:256], GT[:, h:h + 1], 0.0,
                     ALU.subtract, ALU.min)
                p.act(xe[:], x1[:], AF.Exp)
                p.tt("dve", ntf[:], xe[:], pA[:, 0:128], ALU.mult)
                p.tt("dve", NT0[:], ntf[:], m_su_neg[:], ALU.mult)
                p.trans(ptr[:], NT0[:], identb[:])
                p.copy("dve", N0[:], ptr[:])
                Xs = [N0[:]] ; Ps = [NT0[:]]
                for i in range(6):
                    Pi, Xi = Ps[i], Xs[i]
                    p.mm(pB[:, 0:128], Xi, Pi)
                    pdst = Pb[:, i * 128:(i + 1) * 128]
                    p.copy("dve", pdst, pB[:, 0:128])
                    Ps.append(pdst)
                    if i < 5:
                        p.mm(pB[:, 128:256], Pi, Xi)
                        xdst = Xb[:, i * 128:(i + 1) * 128]
                        p.copy("dve", xdst, pB[:, 128:256])
                        Xs.append(xdst)
                p.trans(ptr[:], vt, identb[:])
                p.copy("dve", Vtm[:], ptr[:])
                p.ts("dve", bV[:], Vtm[:], bT[:, h:h + 1], None, ALU.mult)
                p.mm(pB[:, 128:256], kf, Sh)
                p.stt("dve", tmpf[:], pB[:, 128:256], EGLBn[:, h:h + 1], bV[:],
                      ALU.mult, ALU.add)
                p.copy("pool", tmpb[:], tmpf[:])
                for i in range(6, -1, -1):
                    p.mm(pB[:, 0:128], Ps[i], tmpb[:])
                    p.tt("dve", tmpf[:], pB[:, 0:128], tmpf[:], ALU.add)
                    p.copy("pool", tmpb[:], tmpf[:])
                p.mm(pB[:, 256:384], qf, Sh)
                p.ts("dve", O1[:], pB[:, 256:384], EG[:, h:h + 1], None, ALU.mult)
                p.mm(pA[:, 256:384], kb, qb)
                p.load(grow[:], G_d[b, h:h + 1, c0:c0 + 128])
                p.mm(pA[:, 384:512], onesr[:], grow[:])
                p.ts("dve", x1[:], pA[:, 384:512], GT[:, h:h + 1], 0.0,
                     ALU.subtract, ALU.min)
                p.act(xe[:], x1[:], AF.Exp)
                p.tt("dve", dqm[:], xe[:], m_su_inc[:], ALU.mult)
                p.tt("dve", ntf[:], dqm[:], pA[:, 256:384], ALU.mult)
                p.copy("pool", NT0[:], ntf[:])
                p.mm(pB[:, 0:128], NT0[:], tmpb[:])
                p.tt("dve", Osb[:], pB[:, 0:128], O1[:], ALU.add)
                p.ts("dve", Ut[:], tmpf[:], dqm[:, 127:128], None, ALU.mult)
                p.trans(ptf[:], kf, ident[:])
                p.copy("dve", khtm[:], ptf[:])
                p.mm(pB[:, 384:512], khtm[:], Ut[:])
                p.mm(pC[:, 0:1], onesr[:], egl_r[:, h:h + 1])
                p.copy("dve", sfs[:], pC[:, 0:1])
                p.stt("dve", Sh, Sh, sfs[:, 0:1], pB[:, 384:512], ALU.mult, ALU.add)
                p.trans(ptf[:], Osb[:], ident[:])
                p.copy("dve", OT[:, h * 128:(h + 1) * 128], ptf[:])
            zldv = mx[:, 4096:8192]
            p.load(zldv.rearrange("p (ct t) -> p ct t", t=128),
                   zT_d[b].rearrange("(ct p) t -> p ct t", p=128)[:, :, c0:c0 + 128])
            zs = AR[:, 0:4096]
            p.act(zs, zldv, AF.Silu)
            p.tt("dve", zs, OT[:], zs, ALU.mult)
            sq = AR[:, 4096:8192]
            gN = cv[:, 0:4096]
            p.act(sq, zs, AF.Square)
            for s8 in range(8):
                p.mm(pC[0:1, :], onesc[:], sq[:, s8 * 512:(s8 + 1) * 512])
                p.copy("dve", ssqr[:], pC[0:1, :])
                p.act(ssqr[:], ssqr[:], AF.Ln, scale=1.0 / 128, bias=1e-6)
                p.act(rr[:], ssqr[:], AF.Exp, scale=-0.5)
                p.mm(pC[:], onesr[:], rr[:])
                p.stt("dve", gN[:, s8 * 512:(s8 + 1) * 512],
                      zs[:, s8 * 512:(s8 + 1) * 512], nws[:, 0:1], pC[:],
                      ALU.mult, ALU.mult)
            p.store(gatedT_d[b].rearrange("(ct p) t -> p ct t", p=128)[:, :, c0:c0 + 128],
                    gN.rearrange("p (ct t) -> p ct t", t=128))
        wbf = AR[:].bitcast(BF16)
        for qh in range(4):
            o0 = qh * 512
            for ct in range(32):
                p.load(scr[:, 0:512], Wout_d[ct * 128:(ct + 1) * 128, o0:o0 + 512])
                p.copy("pool", wbf[:, ct * 512:(ct + 1) * 512], scr[:, 0:512])
            atile = mx[:, 0:4096]
            for tc in range(16):
                p.load(atile.rearrange("p (ct t) -> p ct t", t=128),
                       gatedT_d[b].rearrange("(ct p) t -> p ct t", p=128)[:, :, tc * 128:(tc + 1) * 128])
                for ct in range(32):
                    p.mm(pD[:], atile[:, ct * 128:(ct + 1) * 128],
                         wbf[:, ct * 512:(ct + 1) * 512],
                         start=(ct == 0), stop=(ct == 31))
                p.copy("dve", y_t[:, 0:512], pD[:])
                p.store(y_d[b, tc * 128:(tc + 1) * 128, o0:o0 + 512], y_t[:, 0:512])
        yln = AR[:, 0:2048]
        xres = scr
        for tt in range(16):
            r0 = tok0 + tt * 128
            p.load(y_t[:], y_d[b, tt * 128:(tt + 1) * 128, :])
            ln_tile(ctx, yln, y_t[:], 2048)
            p.load(xres[:], x_in[r0:r0 + 128, :])
            p.tt("dve", yln, yln, xres[:], ALU.add)
            p.store(x_out[r0:r0 + 128, :], yln)


def build_s1a_program():
    nc, p, ctx = _new_prog()
    N, D = 4096, 2048
    x_in = nc.dram_tensor("x_in", [N, D], F32, kind="ExternalInput").ap()
    Wqkvz = nc.dram_tensor("W_qkvz", [2048, 12288], F32, kind="ExternalInput").ap()
    Wba = nc.dram_tensor("W_ba", [2048, 64], F32, kind="ExternalInput").ap()
    Alog = nc.dram_tensor("A_log", [32], F32, kind="ExternalInput").ap()
    dtb = nc.dram_tensor("dt_bias", [32], F32, kind="ExternalInput").ap()
    mixedT = nc.dram_tensor("mixedT_d", [2, 8192, 2048], BF16, kind="ExternalOutput").ap()
    zT = nc.dram_tensor("zT_d", [2, 4096, 2048], BF16, kind="ExternalOutput").ap()
    G_d = nc.dram_tensor("G_d", [2, 32, 2048], F32, kind="ExternalOutput").ap()
    lnb_d = nc.dram_tensor("lnb_d", [2, 32, 2048], F32, kind="ExternalOutput").ap()
    stage_s1a(ctx, x_in, Wqkvz, Wba, Alog, dtb, mixedT, zT, G_d, lnb_d)
    p.barrier()
    return p.build()


def build_s1b_program():
    nc, p, ctx = _new_prog()
    N, D = 4096, 2048
    x_in = nc.dram_tensor("x_in", [N, D], F32, kind="ExternalInput").ap()
    mixedT = nc.dram_tensor("mixedT_d", [2, 8192, 2048], BF16, kind="ExternalInput").ap()
    zT = nc.dram_tensor("zT_d", [2, 4096, 2048], BF16, kind="ExternalInput").ap()
    G_d = nc.dram_tensor("G_d", [2, 32, 2048], F32, kind="ExternalInput").ap()
    lnb_d = nc.dram_tensor("lnb_d", [2, 32, 2048], F32, kind="ExternalInput").ap()
    convw = nc.dram_tensor("conv_w", [8192, 4], F32, kind="ExternalInput").ap()
    normw = nc.dram_tensor("norm_w", [128], F32, kind="ExternalInput").ap()
    Wout = nc.dram_tensor("W_out", [4096, 2048], F32, kind="ExternalInput").ap()
    x_out = nc.dram_tensor("x_out", [N, D], F32, kind="ExternalOutput").ap()
    gatedT = nc.dram_tensor("gatedT_d", [2, 4096, 2048], BF16, kind="Internal").ap()
    y_d = nc.dram_tensor("y_d", [2, 2048, 2048], F32, kind="Internal").ap()
    stage_s1b(ctx, x_in, mixedT, zT, G_d, lnb_d, convw, normw, Wout,
              x_out, gatedT, y_d)
    p.barrier()
    return p.build()



def kernel_multilaunch(L0_attn_hidden_states, L0_attn_W_qkvz, L0_attn_W_ba, L0_attn_conv_w,
           L0_attn_dt_bias, L0_attn_A_log, L0_attn_norm_w, L0_attn_W_out,
           L0_ffn_gate_up_packed, L0_ffn_gate_up_scales, L0_ffn_down_packed,
           L0_ffn_down_scales, L1_attn_Wq, L1_attn_Wk, L1_attn_Wv, L1_attn_Wo,
           L1_ffn_w_gate, L1_ffn_w_up, L1_ffn_w_down, L1_ffn_s_gate,
           L1_ffn_s_up, L1_ffn_s_down):
    x = np.ascontiguousarray(np.asarray(L0_attn_hidden_states, np.float32))
    B, T, H = x.shape
    consts = {nm: arr for nm, arr in make_consts().items() if not nm.endswith("_bf")}
    s3c = make_s3_consts()
    shards = [np.ascontiguousarray(x[2 * c:2 * c + 2].reshape(2 * T, H))
              for c in range(NCORES)]
    cores = list(range(NCORES))

    def launch(build_fn, per_core_extra):
        nc = build_fn()
        in_maps = []
        for c in cores:
            m = dict(consts)
            m.update(per_core_extra(c))
            in_maps.append(m)
        res = run_bass_kernel_spmd(nc, in_maps, core_ids=cores)
        return res.results

    w1 = {"W_qkvz": np.asarray(L0_attn_W_qkvz, np.float32),
          "W_ba": np.asarray(L0_attn_W_ba, np.float32),
          "A_log": np.asarray(L0_attn_A_log, np.float32),
          "dt_bias": np.asarray(L0_attn_dt_bias, np.float32)}
    r1 = launch(build_s1a_program, lambda c: dict(w1, x_in=shards[c]))
    w2 = {"conv_w": np.asarray(L0_attn_conv_w, np.float32),
          "norm_w": np.asarray(L0_attn_norm_w, np.float32),
          "W_out": np.asarray(L0_attn_W_out, np.float32)}
    r2 = launch(build_s1b_program, lambda c: dict(
        w2, x_in=shards[c], mixedT_d=r1[c]["mixedT_d"], zT_d=r1[c]["zT_d"],
        G_d=r1[c]["G_d"], lnb_d=r1[c]["lnb_d"]))
    x1 = [np.asarray(r2[c]["x_out"]) for c in cores]
    w3 = {"gup": np.asarray(L0_ffn_gate_up_packed, np.int32),
          "gus": np.asarray(L0_ffn_gate_up_scales, np.float16),
          "dnp": np.asarray(L0_ffn_down_packed, np.int32),
          "dns": np.asarray(L0_ffn_down_scales, np.float16)}
    r3 = launch(build_s2_program, lambda c: dict(w3, x_in=x1[c]))
    x2 = [np.asarray(r3[c]["x_out"]) for c in cores]
    w4 = {"Wq": np.asarray(L1_attn_Wq, np.float32),
          "Wk": np.asarray(L1_attn_Wk, np.float32),
          "Wv": np.asarray(L1_attn_Wv, np.float32),
          "Wo": np.asarray(L1_attn_Wo, np.float32)}
    w4.update(s3c)
    r4 = launch(build_s3_program, lambda c: dict(w4, x_in=x2[c]))
    x3 = [np.asarray(r4[c]["x_out"]) for c in cores]
    w5 = {"w_gate": np.asarray(L1_ffn_w_gate, np.int8),
          "w_up": np.asarray(L1_ffn_w_up, np.int8),
          "w_down": np.asarray(L1_ffn_w_down, np.int8),
          "s_gate": np.asarray(L1_ffn_s_gate, np.float32),
          "s_up": np.asarray(L1_ffn_s_up, np.float32),
          "s_down": np.asarray(L1_ffn_s_down, np.float32)}
    r5 = launch(build_s4_program, lambda c: dict(w5, x_in=x3[c]))
    out = np.concatenate([np.asarray(r5[c]["x_out"], np.float32).reshape(2, T, H)
                          for c in cores], 0)
    return out



def build_fused_program():
    nc, p, ctx = _new_prog()
    N, D = 4096, 2048
    E = "ExternalInput"
    x_in = nc.dram_tensor("x_in", [N, D], F32, kind=E).ap()
    Wqkvz = nc.dram_tensor("W_qkvz", [2048, 12288], F32, kind=E).ap()
    Wba = nc.dram_tensor("W_ba", [2048, 64], F32, kind=E).ap()
    Alog = nc.dram_tensor("A_log", [32], F32, kind=E).ap()
    dtb = nc.dram_tensor("dt_bias", [32], F32, kind=E).ap()
    convw = nc.dram_tensor("conv_w", [8192, 4], F32, kind=E).ap()
    normw = nc.dram_tensor("norm_w", [128], F32, kind=E).ap()
    Wout = nc.dram_tensor("W_out", [4096, 2048], F32, kind=E).ap()
    gup = nc.dram_tensor("gup", [11264, 256], I32, kind=E).ap()
    gus = nc.dram_tensor("gus", [16, 11264], F16, kind=E).ap()
    dnp = nc.dram_tensor("dnp", [2048, 704], I32, kind=E).ap()
    dns = nc.dram_tensor("dns", [44, 2048], F16, kind=E).ap()
    Wq = nc.dram_tensor("Wq", [D, D], F32, kind=E).ap()
    Wk = nc.dram_tensor("Wk", [D, D], F32, kind=E).ap()
    Wv = nc.dram_tensor("Wv", [D, D], F32, kind=E).ap()
    Wo = nc.dram_tensor("Wo", [D, D], F32, kind=E).ap()
    cosT = nc.dram_tensor("c_cosT", [64, 2048], F32, kind=E).ap()
    sinT = nc.dram_tensor("c_sinT", [64, 2048], F32, kind=E).ap()
    wg = nc.dram_tensor("w_gate", [D, 8192], I8, kind=E).ap()
    wu = nc.dram_tensor("w_up", [D, 8192], I8, kind=E).ap()
    wd = nc.dram_tensor("w_down", [8192, D], I8, kind=E).ap()
    sg = nc.dram_tensor("s_gate", [8192], F32, kind=E).ap()
    su = nc.dram_tensor("s_up", [8192], F32, kind=E).ap()
    sd = nc.dram_tensor("s_down", [D], F32, kind=E).ap()
    x_out = nc.dram_tensor("x_out", [N, D], F32, kind="ExternalOutput").ap()
    I_ = "Internal"
    mixedT = nc.dram_tensor("mixedT_d", [2, 8192, 2048], BF16, kind=I_).ap()
    zT = nc.dram_tensor("zT_d", [2, 4096, 2048], BF16, kind=I_).ap()
    G_d = nc.dram_tensor("G_d", [2, 32, 2048], F32, kind=I_).ap()
    lnb_d = nc.dram_tensor("lnb_d", [2, 32, 2048], F32, kind=I_).ap()
    gatedT = nc.dram_tensor("gatedT_d", [2, 4096, 2048], BF16, kind=I_).ap()
    y_d = nc.dram_tensor("y_d", [2, 2048, 2048], F32, kind=I_).ap()
    x1_d = nc.dram_tensor("x1_d", [N, D], F32, kind=I_).ap()
    x2_d = nc.dram_tensor("x2_d", [N, D], F32, kind=I_).ap()
    x3_d = nc.dram_tensor("x3_d", [N, D], F32, kind=I_).ap()
    WguT = nc.dram_tensor("WguT", [2048, 11264], BF16, kind=I_).ap()
    WdT = nc.dram_tensor("WdT", [5632, 2048], BF16, kind=I_).ap()
    fused0 = nc.dram_tensor("fused0_d", [N, 5632], BF16, kind=I_).ap()
    WqT = nc.dram_tensor("WqT", [D, D], BF16, kind=I_).ap()
    WkT = nc.dram_tensor("WkT", [D, D], BF16, kind=I_).ap()
    WvT = nc.dram_tensor("WvT", [D, D], BF16, kind=I_).ap()
    WoT = nc.dram_tensor("WoT", [D, D], BF16, kind=I_).ap()
    attnT = nc.dram_tensor("attnT_d", [2, D, 2048], BF16, kind=I_).ap()
    yat = nc.dram_tensor("yat_d", [2, 2048, 2048], F32, kind=I_).ap()
    fused1 = nc.dram_tensor("fused_d", [N, 8192], BF16, kind=I_).ap()

    stage_s1a(ctx, x_in, Wqkvz, Wba, Alog, dtb, mixedT, zT, G_d, lnb_d)
    p.barrier(); ctx.free_stage_sbuf()
    stage_s1b(ctx, x_in, mixedT, zT, G_d, lnb_d, convw, normw, Wout,
              x1_d, gatedT, y_d)
    p.barrier(); ctx.free_stage_sbuf()
    stage_l0ffn(ctx, x1_d, gup, gus, dnp, dns, x2_d, WguT, WdT, fused0)
    p.barrier(); ctx.free_stage_sbuf()
    stage_l1attn(ctx, x2_d, Wq, Wk, Wv, Wo, cosT, sinT, x3_d,
                 WqT, WkT, WvT, WoT, attnT, yat)
    p.barrier(); ctx.free_stage_sbuf()
    stage_l1ffn(ctx, x3_d, wg, wu, wd, sg, su, sd, x_out, fused1)
    p.barrier()
    return p.build()


def kernel_fused(**inputs):
    x = np.ascontiguousarray(np.asarray(inputs["L0_attn_hidden_states"], np.float32))
    B, T, H = x.shape
    consts = {nm: arr for nm, arr in make_consts().items() if not nm.endswith("_bf")}
    consts.update(make_s3_consts())
    shards = [np.ascontiguousarray(x[2 * c:2 * c + 2].reshape(2 * T, H))
              for c in range(NCORES)]
    w = {"W_qkvz": np.asarray(inputs["L0_attn_W_qkvz"], np.float32),
         "W_ba": np.asarray(inputs["L0_attn_W_ba"], np.float32),
         "A_log": np.asarray(inputs["L0_attn_A_log"], np.float32),
         "dt_bias": np.asarray(inputs["L0_attn_dt_bias"], np.float32),
         "conv_w": np.asarray(inputs["L0_attn_conv_w"], np.float32),
         "norm_w": np.asarray(inputs["L0_attn_norm_w"], np.float32),
         "W_out": np.asarray(inputs["L0_attn_W_out"], np.float32),
         "gup": np.asarray(inputs["L0_ffn_gate_up_packed"], np.int32),
         "gus": np.asarray(inputs["L0_ffn_gate_up_scales"], np.float16),
         "dnp": np.asarray(inputs["L0_ffn_down_packed"], np.int32),
         "dns": np.asarray(inputs["L0_ffn_down_scales"], np.float16),
         "Wq": np.asarray(inputs["L1_attn_Wq"], np.float32),
         "Wk": np.asarray(inputs["L1_attn_Wk"], np.float32),
         "Wv": np.asarray(inputs["L1_attn_Wv"], np.float32),
         "Wo": np.asarray(inputs["L1_attn_Wo"], np.float32),
         "w_gate": np.asarray(inputs["L1_ffn_w_gate"], np.int8),
         "w_up": np.asarray(inputs["L1_ffn_w_up"], np.int8),
         "w_down": np.asarray(inputs["L1_ffn_w_down"], np.int8),
         "s_gate": np.asarray(inputs["L1_ffn_s_gate"], np.float32),
         "s_up": np.asarray(inputs["L1_ffn_s_up"], np.float32),
         "s_down": np.asarray(inputs["L1_ffn_s_down"], np.float32)}
    w.update(consts)
    nc = build_fused_program()
    in_maps = [dict(w, x_in=shards[c]) for c in range(NCORES)]
    res = run_bass_kernel_spmd(nc, in_maps, core_ids=list(range(NCORES)))
    out = np.concatenate([np.asarray(res.results[c]["x_out"], np.float32)
                          .reshape(2, T, H) for c in range(NCORES)], 0)
    return out


def kernel(**inputs):
    return kernel_fused(**inputs)


# version 05 (final)
# speedup vs baseline: 1.0064x; 1.0064x over previous
import numpy as np
import concourse.bass as bass
import concourse.mybir as mybir
from concourse.bass_utils import run_bass_kernel_spmd

F32 = mybir.dt.float32
BF16 = mybir.dt.bfloat16
F16 = mybir.dt.float16
I32 = mybir.dt.int32
I8 = mybir.dt.int8
AF = mybir.ActivationFunctionType
ALU = mybir.AluOpType
AX = mybir.AxisListType

NCORES = 8


class Prog:

    COMPUTE = ("pe", "act", "dve", "pool")
    STREAMS = ("pe", "act", "dve", "pool", "qload", "qstore")

    DMA_SLOTS = 4

    def __init__(self, nc):
        self.nc = nc
        self.ops = []
        self.epoch = 0
        self.cnt = {s: 0 for s in self.STREAMS}
        for q in ("qload", "qstore"):
            for k in range(self.DMA_SLOTS):
                self.cnt[(q, k)] = 0
        self.dma_rr = {"qload": 0, "qstore": 0}
        self.last_w = {}
        self.readers = {}
        self.satisfied = {s: {t: 0 for t in self.STREAMS} for s in self.STREAMS}
        self._names = {}
        self._const_t = None
        self._consts = {}

    def constf(self, value, parts=128):
        if self._const_t is None:
            self._const_t = self.nc.alloc_sbuf_tensor("constpool", [128, 64], F32)
        if value not in self._consts:
            i = len(self._consts)
            self.memset("pool", self._const_t[:, i:i + 1], float(value))
            self._consts[value] = i
        i = self._consts[value]
        return self._const_t[0:parts, i:i + 1]

    def _key(self, ap):
        t = getattr(ap, "tensor", ap)
        return getattr(t, "name", str(id(t)))

    def add(self, stream, emit, reads=(), writes=(), dma=False):
        inc = 16 if dma else 1
        if dma:
            slot = self.dma_rr[stream]
            self.dma_rr[stream] = (slot + 1) % self.DMA_SLOTS
            skey = (stream, slot)
        else:
            skey = stream
        waits = {}
        for ap in list(reads) + list(writes):
            k = self._key(ap)
            lw = self.last_w.get(k)
            if lw is not None:
                s, c = lw
                if not (s == "pe" and skey == "pe"):
                    waits[s] = max(waits.get(s, 0), c)
        for ap in writes:
            k = self._key(ap)
            for s, c in self.readers.get(k, {}).items():
                if not (s == "pe" and skey == "pe"):
                    waits[s] = max(waits.get(s, 0), c)
        sat = self.satisfied.setdefault(skey, {})
        waits = {s: c for s, c in waits.items() if c > sat.get(s, 0)}
        for s, c in waits.items():
            sat[s] = max(sat.get(s, 0), c)
        self.cnt[skey] += inc
        myc = self.cnt[skey]
        self.ops.append((skey, emit, waits, inc, self.epoch))
        for ap in writes:
            k = self._key(ap)
            self.last_w[k] = (skey, myc)
            self.readers[k] = {}
        for ap in reads:
            k = self._key(ap)
            self.readers.setdefault(k, {})[skey] = myc
        return myc

    def barrier(self):
        snap = {k: c for k, c in self.cnt.items() if c > 0}
        for s in ("pe", "act", "dve", "pool", "qload", "qstore"):
            self.ops.append((s if isinstance(s, str) and not s.startswith("q")
                             else (s, None) if False else s,
                             None, dict(snap), 0, self.epoch))
        self.epoch += 1
        for k in list(self.cnt):
            if isinstance(k, str):
                self.cnt[k] = 0
        self.satisfied = {}
        self.last_w = {}
        self.readers = {}

    def load(self, dst, src, **kw):
        q = "qload"
        try:
            if dst.tensor.dtype != src.tensor.dtype:
                q = "qstore"
        except AttributeError:
            pass
        self.add(q, lambda e: e.dma_start(out=dst, in_=src, **kw),
                 reads=[src], writes=[dst], dma=True)

    def store(self, dst, src, **kw):
        self.add("qstore", lambda e: e.dma_start(out=dst, in_=src, **kw),
                 reads=[src], writes=[dst], dma=True)

    def mm(self, out, lhsT, rhs, start=True, stop=True):
        self.add("pe", lambda e: e.matmul(out, lhsT, rhs, start=start, stop=stop),
                 reads=[lhsT, rhs], writes=[out])

    def trans(self, out, in_, ident):
        self.add("pe", lambda e: e.transpose(out, in_, ident),
                 reads=[in_, ident], writes=[out])

    def act(self, out, in_, func, bias=0.0, scale=1.0, accum_out=None):
        if isinstance(bias, float) and bias != 0.0:
            bias = self.constf(bias, parts=out.shape[0])
        r = [in_]
        if isinstance(bias, bass.AP):
            r.append(bias)
        if isinstance(scale, bass.AP):
            r.append(scale)
        w = [out] + ([accum_out] if accum_out is not None else [])
        self.add("act", lambda e: e.activation(out, in_, func, bias=bias, scale=scale,
                                               accum_out=accum_out),
                 reads=r, writes=w)

    def ew(self, stream, name, *args, reads=(), writes=(), **kw):
        self.add(stream, lambda e: getattr(e, name)(*args, **kw),
                 reads=reads, writes=writes)

    def tt(self, stream, out, a, b, op):
        self.add(stream, lambda e: e.tensor_tensor(out, a, b, op),
                 reads=[a, b], writes=[out])

    def ts(self, stream, out, in0, s1, s2, op0, op1=ALU.bypass, accum_out=None):
        r = [in0]
        if isinstance(s1, bass.AP):
            r.append(s1)
        if isinstance(s2, bass.AP):
            r.append(s2)
        w = [out] + ([accum_out] if accum_out is not None else [])
        self.add(stream, lambda e: e.tensor_scalar(out, in0, s1, s2, op0, op1,
                                                   accum_out=accum_out),
                 reads=r, writes=w)

    def stt(self, stream, out, in0, scalar, in1, op0, op1):
        r = [in0, in1]
        if isinstance(scalar, bass.AP):
            r.append(scalar)
        self.add(stream, lambda e: e.scalar_tensor_tensor(out, in0, scalar, in1, op0, op1),
                 reads=r, writes=[out])

    def copy(self, stream, out, in_):
        self.add(stream, lambda e: e.tensor_copy(out, in_), reads=[in_], writes=[out])

    def memset(self, stream, ap, val):
        self.add(stream, lambda e: e.memset(ap, val), reads=[], writes=[ap])

    def reduce(self, stream, out, in_, op, axis=None):
        axis = axis if axis is not None else AX.X
        self.add(stream, lambda e: e.tensor_reduce(out, in_, axis, op),
                 reads=[in_], writes=[out])

    def build(self):
        nc = self.nc
        import contextlib
        stack = contextlib.ExitStack()
        n_epochs = self.epoch + 1
        sems = {}
        for ep in range(n_epochs):
            for st in self.COMPUTE:
                sems[(ep, st)] = stack.enter_context(nc.semaphore(f"sem{ep}_{st}"))
        for q in ("qload", "qstore"):
            for k in range(self.DMA_SLOTS):
                sems[(q, k)] = stack.enter_context(nc.semaphore(f"dsem_{q}{k}"))

        def sem_of(key, ep):
            return sems[key] if isinstance(key, tuple) else sems[(ep, key)]

        def eng_name_of(key):
            base = key[0] if isinstance(key, tuple) else key
            return {"pe": "tensor", "act": "scalar", "dve": "vector",
                    "pool": "gpsimd", "qload": "sync", "qstore": "gpsimd"}[base]

        with nc.Block() as blk:
            for eng_name in ("tensor", "scalar", "vector", "gpsimd", "sync"):
                ops = [op for op in self.ops if eng_name_of(op[0]) == eng_name]
                if not ops:
                    continue

                def body(eng, ops=ops):
                    for (skey, emit, waits, inc, ep) in ops:
                        if emit is None and waits and skey in self.COMPUTE:
                            eng.drain()
                        for s, c in sorted(waits.items(), key=str):
                            eng.wait_ge(sem_of(s, ep), c)
                        if emit is not None:
                            inst = emit(eng)
                            inst.then_inc(sem_of(skey, ep), inc)
                getattr(blk, eng_name)(body)
        stack.close()
        return nc



class Ctx:

    def __init__(self, nc, p):
        self.nc = nc
        self.p = p
        self.sb = {}
        self.sb_guard = {}
        self.ps = {}

    _uid = 0

    def S(self, name, shape, dt=F32):
        if name not in self.sb:
            Ctx._uid += 1
            g = self.nc.sbuf_tensor(f"{name}__{Ctx._uid}", shape, dt)
            self.sb[name] = g.__enter__()
            self.sb_guard[name] = g
        return self.sb[name]

    def free_stage_sbuf(self):
        for name in reversed(list(self.sb)):
            if name.startswith("s_c_"):
                continue
            self.sb_guard[name].__exit__(None, None, None)
            del self.sb[name]
            del self.sb_guard[name]

    def P(self, name, shape, dt=F32):
        if name not in self.ps:
            self.ps[name] = self.nc.alloc_psum_tensor(name, shape, dt)
        return self.ps[name]


def make_consts():
    ident = np.eye(128, dtype=np.float32)
    c = {
        "c_ident": ident,
        "c_ident_bf": ident,
        "c_ones_r": np.ones((1, 128), np.float32),
        "c_ones_c": np.ones((128, 1), np.float32),
        "c_mask_su_incl": np.triu(np.ones((128, 128), np.float32)),
        "c_mask_su_strict_neg": -np.triu(np.ones((128, 128), np.float32), 1),
    }
    return c


def declare_consts(ctx, dram_inputs):
    p = ctx.p
    for nm, shape, dt in [("c_ident", [128, 128], F32),
                          ("c_ones_r", [1, 128], F32),
                          ("c_ones_c", [128, 1], F32),
                          ("c_mask_su_incl", [128, 128], F32),
                          ("c_mask_su_strict_neg", [128, 128], F32)]:
        t = ctx.S("s_" + nm, shape, dt)
        p.load(t[:], dram_inputs[nm])
    for nm in ["c_ident", "c_ones_r", "c_ones_c"]:
        t = ctx.S("s_" + nm + "_bf", [128, ctx.sb["s_" + nm].shape[1]] if False else list(ctx.sb["s_" + nm].shape), BF16)
        p.copy("pool", t[:], ctx.sb["s_" + nm][:])


def ln_tile(ctx, out_ap, in_ap, D, pref="ln", eps=1e-5):
    p = ctx.p
    s1 = ctx.S(pref + "_s1", [128, 1])
    s2 = ctx.S(pref + "_s2", [128, 1])
    nm = ctx.S(pref + "_nm", [128, 1])
    e2 = ctx.S(pref + "_e2", [128, 1])
    nv = ctx.S(pref + "_nv", [128, 1])
    l = ctx.S(pref + "_l", [128, 1])
    rstd = ctx.S(pref + "_rstd", [128, 1])
    bb = ctx.S(pref + "_bb", [128, 1])
    scr = ctx.S(pref + "_scr", [128, D])
    p.reduce("dve", s1[:], in_ap, ALU.add)
    p.act(scr[:], in_ap, AF.Square, accum_out=s2[:])
    p.ew("act", "mul", nm[:], s1[:], -1.0 / D, reads=[s1], writes=[nm])
    p.ew("act", "mul", e2[:], s2[:], 1.0 / D, reads=[s2], writes=[e2])
    p.stt("dve", nv[:], nm[:], nm[:, 0:1], e2[:], ALU.mult, ALU.subtract)
    p.act(l[:], nv[:], AF.Ln, scale=-1.0, bias=eps)
    p.act(rstd[:], l[:], AF.Exp, scale=-0.5)
    p.tt("dve", bb[:], nm[:], rstd[:], ALU.mult)
    p.act(out_ap, in_ap, AF.Identity, bias=bb[:], scale=rstd[:])


def transpose128(ctx, dst_ap, src_ap, dt=BF16, pbank="ptr"):
    p = ctx.p
    ident = ctx.sb["s_c_ident_bf" if dt == BF16 else "s_c_ident"]
    ps = ctx.P(pbank, [128, 128], dt)
    p.trans(ps[:], src_ap, ident[:])
    p.copy("dve", dst_ap, ps[:])


def row_bcast(ctx, dst_sb_ap, row_ap, n, dt=F32, pbank="pbc"):
    p = ctx.p
    ones = ctx.sb["s_c_ones_r_bf" if dt == BF16 else "s_c_ones_r"]
    ps = ctx.P(pbank, [128, 512], F32)
    p.mm(ps[:, 0:n], ones[:], row_ap)
    p.copy("dve", dst_sb_ap, ps[:, 0:n])



def stage_l1ffn(ctx, x_in, w_gate, w_up, w_down, s_gate, s_up, s_down,
                x_out, fused_d):
    p = ctx.p
    N, D, I = 4096, 2048, 8192
    A1 = ctx.S("ffn_A1", [128, 32768], BF16)
    A2 = ctx.S("ffn_A2", [128, 16384], BF16)
    I1 = ctx.S("ffn_I1", [128, 16384], I8)
    xt = ctx.S("ffn_xt", [128, 2048])
    xn = ctx.S("ffn_xn", [128, 2048], BF16)
    srow = ctx.S("ffn_srow", [1, 512])
    sbc_g = ctx.S("ffn_sbc_g", [128, 512])
    sbc_u = ctx.S("ffn_sbc_u", [128, 512])
    sdbc = ctx.S("ffn_sdbc", [128, 2048])
    silu_t = ctx.S("ffn_silu", [128, 512])
    su_t = ctx.S("ffn_su", [128, 512])
    fus_t = ctx.S("ffn_fus", [128, 512], BF16)
    y0 = ctx.S("ffn_y0", [128, 2048])
    y1 = ctx.S("ffn_y1", [128, 2048])
    yln = ctx.S("ffn_yln", [128, 2048])
    xres = ctx.S("ffn_xres", [128, 2048])
    pg = ctx.P("pp_g", [128, 512])
    pu = ctx.P("pp_u", [128, 512])
    pd0 = ctx.P("pp_d0", [128, 512])
    pd1 = ctx.P("pp_d1", [128, 512])

    w_gate_r = w_gate.rearrange("(ht p) o -> p ht o", p=128)
    w_up_r = w_up.rearrange("(ht p) o -> p ht o", p=128)
    w_down_r = w_down.rearrange("(ih it p) o -> ih p it o", it=32, p=128)
    s_gate_r = s_gate.rearrange("(a o) -> a o", o=512)
    s_up_r = s_up.rearrange("(a o) -> a o", o=512)
    s_down_r = s_down.rearrange("(a o) -> a o", o=512)

    for half in range(2):
        tok0 = half * 2048
        xnT = A1
        for tt in range(16):
            r0 = tok0 + tt * 128
            p.load(xt[:], x_in[r0:r0 + 128, :])
            ln_tile(ctx, xn[:], xt[:], D)
            for ht in range(16):
                transpose128(ctx, xnT[:, ht * 2048 + tt * 128: ht * 2048 + (tt + 1) * 128],
                             xn[:, ht * 128:(ht + 1) * 128])
        for oc in range(16):
            for wi, (W_r, s_r, wbf, sbc) in enumerate(
                    [(w_gate_r, s_gate_r, A2[:, :8192], sbc_g),
                     (w_up_r, s_up_r, A2[:, 8192:], sbc_u)]):
                wi8v = I1[:, :8192].rearrange("p (ht o) -> p ht o", o=512)
                p.load(wi8v, W_r[:, :, oc * 512:(oc + 1) * 512])
                p.copy("dve", wbf, I1[:, :8192])
                p.load(srow[:], s_r[oc:oc + 1, :])
                row_bcast(ctx, sbc[:], srow[:], 512)
            for tt in range(16):
                for ht in range(16):
                    p.mm(pg[:], xnT[:, ht * 2048 + tt * 128: ht * 2048 + (tt + 1) * 128],
                         A2[:, ht * 512:(ht + 1) * 512],
                         start=(ht == 0), stop=(ht == 15))
                for ht in range(16):
                    p.mm(pu[:], xnT[:, ht * 2048 + tt * 128: ht * 2048 + (tt + 1) * 128],
                         A2[:, 8192 + ht * 512: 8192 + (ht + 1) * 512],
                         start=(ht == 0), stop=(ht == 15))
                p.tt("dve", silu_t[:], pg[:], sbc_g[:], ALU.mult)
                p.act(silu_t[:], silu_t[:], AF.Silu)
                p.tt("dve", su_t[:], pu[:], sbc_u[:], ALU.mult)
                p.tt("dve", fus_t[:], silu_t[:], su_t[:], ALU.mult)
                p.store(fused_d[tok0 + tt * 128: tok0 + (tt + 1) * 128,
                                oc * 512:(oc + 1) * 512], fus_t[:])

    for hc in range(4):
        p.load(srow[:], s_down_r[hc:hc + 1, :])
        row_bcast(ctx, sdbc[:, hc * 512:(hc + 1) * 512], srow[:], 512)
    for g in range(16):
        tok0 = g * 256
        fs = A2
        fusT = A1[:, :16384]
        wdbf = A1[:, 16384:]
        for ts in range(2):
            p.load(fs[:, ts * 8192:(ts + 1) * 8192],
                   fused_d[tok0 + ts * 128: tok0 + (ts + 1) * 128, :])
        for ts in range(2):
            for it in range(64):
                transpose128(ctx, fusT[:, it * 256 + ts * 128: it * 256 + (ts + 1) * 128],
                             fs[:, ts * 8192 + it * 128: ts * 8192 + (it + 1) * 128])
        for hc in range(4):
            for ih in range(2):
                wdi8v = I1[:].rearrange("p (it o) -> p it o", o=512)
                p.load(wdi8v, w_down_r[ih, :, :, hc * 512:(hc + 1) * 512])
                p.copy("dve", wdbf, I1[:])
                for ts, pd in ((0, pd0), (1, pd1)):
                    for it in range(32):
                        itg = ih * 32 + it
                        p.mm(pd[:], fusT[:, itg * 256 + ts * 128: itg * 256 + (ts + 1) * 128],
                             wdbf[:, it * 512:(it + 1) * 512],
                             start=(ih == 0 and it == 0), stop=(ih == 1 and it == 31))
            for ts, pd, y in ((0, pd0, y0), (1, pd1, y1)):
                p.tt("dve", y[:, hc * 512:(hc + 1) * 512], pd[:],
                     sdbc[:, hc * 512:(hc + 1) * 512], ALU.mult)
        for ts, y in ((0, y0), (1, y1)):
            r0 = tok0 + ts * 128
            ln_tile(ctx, yln[:], y[:], D)
            p.load(xres[:], x_in[r0:r0 + 128, :])
            p.tt("dve", yln[:], yln[:], xres[:], ALU.add)
            p.store(x_out[r0:r0 + 128, :], yln[:])



def dequant_fp4(ctx, packed_d, scales_d, WT_d, O, W, in_dim, groups):
    p = ctx.p
    A1 = ctx.S("ffn_A1", [128, 32768], BF16)
    A2 = ctx.S("ffn_A2", [128, 16384], BF16)
    A1i = A1[:].bitcast(I32)
    pk = A1i[:, 0:704]
    nb = A1i[:, 704:1408]
    mm_ = A1i[:, 1408:2112]
    sg = A1i[:, 2112:2816]
    t1 = A1i[:, 2816:3520]
    t2 = A1i[:, 3520:4224]
    dqi = A1i[:, 4224:9856]
    dqs = A2[:, 0:5632]
    sc16 = A2[0:44, 5632:5760].bitcast(F16)
    sc32 = A2[0:44, 5760:6016].bitcast(F32)
    scT = A2[:, 6016:6104].bitcast(F32)
    wtile = A2[:, 6104:6232]
    ptf = ctx.P("ptf", [128, 128], F32)
    ident = ctx.sb["s_c_ident"]
    for ot in range(O // 128):
        p.load(pk[:, :W], packed_d[ot * 128:(ot + 1) * 128, :])
        p.load(sc16[:groups, :], scales_d[:, ot * 128:(ot + 1) * 128])
        p.copy("pool", sc32[:groups, :], sc16[:groups, :])
        p.trans(ptf[:, :groups], sc32[:groups, :], ident[:groups, :groups])
        p.copy("dve", scT[:, :groups], ptf[:, :groups])
        for j in range(8):
            p.ts("dve", nb[:, :W], pk[:, :W], 4 * j, 15,
                 ALU.logical_shift_right, ALU.bitwise_and)
            p.ts("dve", mm_[:, :W], nb[:, :W], 7, None, ALU.bitwise_and)
            p.ts("dve", sg[:, :W], nb[:, :W], 3, 31,
                 ALU.logical_shift_right, ALU.logical_shift_left)
            p.ts("dve", t1[:, :W], mm_[:, :W], 0x400000, 0x3F800000 - 2 * 0x400000,
                 ALU.mult, ALU.add)
            p.stt("dve", t2[:, :W], mm_[:, :W], 0x3F000000, t1[:, :W],
                  ALU.mult, ALU.min)
            dqv = dqi.bitcast(I32) if False else dqi
            p.tt("dve", dqv[:, j:in_dim:8], t2[:, :W], sg[:, :W], ALU.bitwise_or)
        for g in range(groups):
            p.ts("dve", dqs[:, g * 128:(g + 1) * 128],
                 dqi[:, g * 128:(g + 1) * 128].bitcast(F32),
                 scT[:, g:g + 1], None, ALU.mult)
        for it in range(in_dim // 128):
            transpose128(ctx, wtile[:, 0:128], dqs[:, it * 128:(it + 1) * 128])
            p.store(WT_d[it * 128:(it + 1) * 128, ot * 128:(ot + 1) * 128], wtile[:, 0:128])


def stage_l0ffn(ctx, x_in, gup_d, gus_d, dnp_d, dns_d, x_out,
                WguT_d, WdT_d, fused_d):
    p = ctx.p
    D, I = 2048, 5632
    dequant_fp4(ctx, gup_d, gus_d, WguT_d, 11264, 256, 2048, 16)
    dequant_fp4(ctx, dnp_d, dns_d, WdT_d, 2048, 704, 5632, 44)

    A1 = ctx.S("ffn_A1", [128, 32768], BF16)
    A2 = ctx.S("ffn_A2", [128, 16384], BF16)
    xt = ctx.S("ffn_xt", [128, 2048])
    xn = ctx.S("ffn_xn", [128, 2048], BF16)
    silu_t = ctx.S("ffn_silu", [128, 512])
    fus_t = ctx.S("ffn_fus", [128, 512], BF16)
    y0 = ctx.S("ffn_y0", [128, 2048])
    y1 = ctx.S("ffn_y1", [128, 2048])
    yln = ctx.S("ffn_yln", [128, 2048])
    xres = ctx.S("ffn_xres", [128, 2048])
    rs1 = ctx.S("rms_1", [128, 1])
    rs2 = ctx.S("rms_2", [128, 1])
    rscr = ctx.S("rms_scr", [128, 2048])
    pg = ctx.P("pp_g", [128, 512])
    pu = ctx.P("pp_u", [128, 512])
    pd0 = ctx.P("pp_d0", [128, 512])
    pd1 = ctx.P("pp_d1", [128, 512])

    WguT_r = WguT_d.rearrange("(ht p) o -> p ht o", p=128)
    WdT_r = WdT_d.rearrange("(ih it p) o -> ih p it o", it=22, p=128)

    for half in range(2):
        tok0 = half * 2048
        xnT = A1
        for tt in range(16):
            r0 = tok0 + tt * 128
            p.load(xt[:], x_in[r0:r0 + 128, :])
            ln_tile(ctx, xn[:], xt[:], D)
            for ht in range(16):
                transpose128(ctx, xnT[:, ht * 2048 + tt * 128: ht * 2048 + (tt + 1) * 128],
                             xn[:, ht * 128:(ht + 1) * 128])
        for oc in range(11):
            wg = A2[:, :8192]
            wu = A2[:, 8192:]
            p.load(wg.rearrange("p (ht o) -> p ht o", o=512),
                   WguT_r[:, :, oc * 512:(oc + 1) * 512])
            p.load(wu.rearrange("p (ht o) -> p ht o", o=512),
                   WguT_r[:, :, 5632 + oc * 512: 5632 + (oc + 1) * 512])
            for tt in range(16):
                for ht in range(16):
                    p.mm(pg[:], xnT[:, ht * 2048 + tt * 128: ht * 2048 + (tt + 1) * 128],
                         wg[:, ht * 512:(ht + 1) * 512],
                         start=(ht == 0), stop=(ht == 15))
                for ht in range(16):
                    p.mm(pu[:], xnT[:, ht * 2048 + tt * 128: ht * 2048 + (tt + 1) * 128],
                         wu[:, ht * 512:(ht + 1) * 512],
                         start=(ht == 0), stop=(ht == 15))
                p.act(silu_t[:], pg[:], AF.Silu)
                p.tt("dve", fus_t[:], silu_t[:], pu[:], ALU.mult)
                p.store(fused_d[tok0 + tt * 128: tok0 + (tt + 1) * 128,
                                oc * 512:(oc + 1) * 512], fus_t[:])

    for g in range(16):
        tok0 = g * 256
        fs = A2
        fusT = A1[:, :11264]
        wdbf = A1[:, 11264: 11264 + 11264]
        for ts_ in range(2):
            p.load(fs[:, ts_ * 5632:(ts_ + 1) * 5632],
                   fused_d[tok0 + ts_ * 128: tok0 + (ts_ + 1) * 128, :])
        for ts_ in range(2):
            for it in range(44):
                transpose128(ctx, fusT[:, it * 256 + ts_ * 128: it * 256 + (ts_ + 1) * 128],
                             fs[:, ts_ * 5632 + it * 128: ts_ * 5632 + (it + 1) * 128])
        for hc in range(4):
            for ih in range(2):
                p.load(wdbf.rearrange("p (it o) -> p it o", o=512),
                       WdT_r[ih, :, :, hc * 512:(hc + 1) * 512])
                for ts_, pd in ((0, pd0), (1, pd1)):
                    for it in range(22):
                        itg = ih * 22 + it
                        p.mm(pd[:], fusT[:, itg * 256 + ts_ * 128: itg * 256 + (ts_ + 1) * 128],
                             wdbf[:, it * 512:(it + 1) * 512],
                             start=(ih == 0 and it == 0), stop=(ih == 1 and it == 21))
            for ts_, pd, y in ((0, pd0, y0), (1, pd1, y1)):
                p.copy("dve", y[:, hc * 512:(hc + 1) * 512], pd[:])
        for ts_, y in ((0, y0), (1, y1)):
            r0 = tok0 + ts_ * 128
            p.act(rscr[:], y[:], AF.Square, accum_out=rs1[:])
            p.act(rs2[:], rs1[:], AF.Ln, scale=1.0 / 2048, bias=1e-6)
            p.act(rs1[:], rs2[:], AF.Exp, scale=-0.5)
            p.ts("dve", y[:], y[:], rs1[:, 0:1], None, ALU.mult)
            ln_tile(ctx, yln[:], y[:], D)
            p.load(xres[:], x_in[r0:r0 + 128, :])
            p.tt("dve", yln[:], yln[:], xres[:], ALU.add)
            p.store(x_out[r0:r0 + 128, :], yln[:])



def _new_prog():
    nc = bass.Bass("TRN2", target_bir_lowering=False)
    p = Prog(nc)
    ctx = Ctx(nc, p)
    dins = {}
    for nm, arr in make_consts().items():
        if nm.endswith("_bf"):
            continue
        dins[nm] = nc.dram_tensor(nm, list(arr.shape), F32, kind="ExternalInput").ap()
    declare_consts(ctx, dins)
    p.constf(1e-5)
    p.barrier()
    return nc, p, ctx


def build_s4_program():
    nc, p, ctx = _new_prog()
    N, D, I = 4096, 2048, 8192
    x_in = nc.dram_tensor("x_in", [N, D], F32, kind="ExternalInput").ap()
    wg = nc.dram_tensor("w_gate", [D, I], I8, kind="ExternalInput").ap()
    wu = nc.dram_tensor("w_up", [D, I], I8, kind="ExternalInput").ap()
    wd = nc.dram_tensor("w_down", [I, D], I8, kind="ExternalInput").ap()
    sg = nc.dram_tensor("s_gate", [I], F32, kind="ExternalInput").ap()
    su = nc.dram_tensor("s_up", [I], F32, kind="ExternalInput").ap()
    sd = nc.dram_tensor("s_down", [D], F32, kind="ExternalInput").ap()
    x_out = nc.dram_tensor("x_out", [N, D], F32, kind="ExternalOutput").ap()
    fused_d = nc.dram_tensor("fused_d", [N, I], BF16, kind="Internal").ap()
    stage_l1ffn(ctx, x_in, wg, wu, wd, sg, su, sd, x_out, fused_d)
    p.barrier()
    return p.build()


def build_s2_program():
    nc, p, ctx = _new_prog()
    N, D, I = 4096, 2048, 5632
    x_in = nc.dram_tensor("x_in", [N, D], F32, kind="ExternalInput").ap()
    gup = nc.dram_tensor("gup", [11264, 256], I32, kind="ExternalInput").ap()
    gus = nc.dram_tensor("gus", [16, 11264], F16, kind="ExternalInput").ap()
    dnp = nc.dram_tensor("dnp", [2048, 704], I32, kind="ExternalInput").ap()
    dns = nc.dram_tensor("dns", [44, 2048], F16, kind="ExternalInput").ap()
    x_out = nc.dram_tensor("x_out", [N, D], F32, kind="ExternalOutput").ap()
    WguT = nc.dram_tensor("WguT", [2048, 11264], BF16, kind="Internal").ap()
    WdT = nc.dram_tensor("WdT", [5632, 2048], BF16, kind="Internal").ap()
    fused_d = nc.dram_tensor("fused_d", [N, I], BF16, kind="Internal").ap()
    stage_l0ffn(ctx, x_in, gup, gus, dnp, dns, x_out, WguT, WdT, fused_d)
    p.barrier()
    return p.build()



def transpose_w(ctx, W_d, WT_d, wpref="wtr"):
    p = ctx.p
    wrow = ctx.S(wpref + "_row", [128, 2048])
    wbf = ctx.S(wpref + "_bf", [128, 2048], BF16)
    wt = ctx.S(wpref + "_t", [128, 128], BF16)
    for rt in range(16):
        p.load(wrow[:], W_d[rt * 128:(rt + 1) * 128, :])
        p.copy("pool", wbf[:], wrow[:])
        for ct in range(16):
            transpose128(ctx, wt[:], wbf[:, ct * 128:(ct + 1) * 128])
            p.store(WT_d[ct * 128:(ct + 1) * 128, rt * 128:(rt + 1) * 128], wt[:])


def stage_l1attn(ctx, x_in, Wq_d, Wk_d, Wv_d, Wo_d, cosT_d, sinT_d, x_out,
                 WqT_d, WkT_d, WvT_d, WoT_d, attnT_d, yat_d):
    p = ctx.p
    D, T, NH = 2048, 2048, 16
    for W_d, WT_d in [(Wq_d, WqT_d), (Wk_d, WkT_d), (Wv_d, WvT_d), (Wo_d, WoT_d)]:
        transpose_w(ctx, W_d, WT_d)

    xnT = ctx.S("at_xnT", [128, 16 * 2048], BF16)
    xt = ctx.S("at_xt", [128, 2048])
    xn = ctx.S("at_xn", [128, 2048], BF16)
    cosT = ctx.S("at_cos", [64, 2048], BF16)
    sinT = ctx.S("at_sin", [64, 2048], BF16)
    p4all = ctx.S("at_p4all", [128, 8192], BF16)
    p.load(cosT[:], cosT_d)
    p.load(sinT[:], sinT_d)
    wql = ctx.S("at_wql", [128, 16 * 512], BF16)
    wkl = wql
    wvl = wql
    qTr = ctx.S("at_qT", [128, 4 * 2048], BF16)
    kTr = ctx.S("at_kT", [128, 4 * 2048], BF16)
    v_sb = ctx.S("at_v", [128, 16 * 512], BF16)
    r1 = ctx.S("at_r1", [64, 512])
    r2 = ctx.S("at_r2", [64, 512])
    p2 = ctx.S("at_p2", [128, 512])
    rden = ctx.S("at_rden", [1, 512])
    rdbc = ctx.S("at_rdbc", [128, 512])
    oT_f = ctx.S("at_oTf", [128, 512], BF16)
    wol = wvl
    y_t = p4all[:, 0:4096].bitcast(F32)
    yln = ctx.S("at_yln", [128, 2048])
    xres = xt
    pqk = ctx.P("pp_g", [128, 512])
    poT = ctx.P("pp_u", [128, 512])
    pden = ctx.P("pp_d0", [128, 512])
    pbc2 = ctx.P("pp_d1", [128, 512])
    mask = ctx.sb["s_c_mask_su_incl"]
    onesc_bf = ctx.sb["s_c_ones_c_bf"]

    WqT_r = WqT_d.rearrange("(ht p) o -> p ht o", p=128)
    WkT_r = WkT_d.rearrange("(ht p) o -> p ht o", p=128)
    WvT_r = WvT_d.rearrange("(ht p) o -> p ht o", p=128)
    WoT_r = WoT_d.rearrange("(ht p) o -> p ht o", p=128)

    for b in range(2):
        tok0 = b * 2048
        for tt in range(16):
            r0 = tok0 + tt * 128
            p.load(xt[:], x_in[r0:r0 + 128, :])
            ln_tile(ctx, xn[:], xt[:], D)
            for ht in range(16):
                transpose128(ctx, xnT[:, ht * 2048 + tt * 128: ht * 2048 + (tt + 1) * 128],
                             xn[:, ht * 128:(ht + 1) * 128])
        for grp in range(4):
            c0 = grp * 512
            for W_r, dst in ((WqT_r, qTr), (WkT_r, kTr)):
                p.load(wql[:].rearrange("p (ht o) -> p ht o", o=512), W_r[:, :, c0:c0 + 512])
                for hh in range(4):
                    for tc in range(4):
                        for ht in range(16):
                            p.mm(pqk[:], wql[:, ht * 512 + hh * 128: ht * 512 + (hh + 1) * 128],
                                 xnT[:, ht * 2048 + tc * 512: ht * 2048 + (tc + 1) * 512],
                                 start=(ht == 0), stop=(ht == 15))
                        cs = cosT[:, tc * 512:(tc + 1) * 512]
                        sn = sinT[:, tc * 512:(tc + 1) * 512]
                        d0 = hh * 2048 + tc * 512
                        p.tt("dve", r1[:], pqk[0:64, :], cs, ALU.mult)
                        p.tt("dve", r2[:], pqk[64:128, :], sn, ALU.mult)
                        p.tt("dve", dst[0:64, d0:d0 + 512], r1[:], r2[:], ALU.subtract)
                        p.tt("dve", r1[:], pqk[0:64, :], sn, ALU.mult)
                        p.tt("dve", r2[:], pqk[64:128, :], cs, ALU.mult)
                        p.tt("dve", dst[64:128, d0:d0 + 512], r1[:], r2[:], ALU.add)
            p.load(wvl[:].rearrange("p (ht o) -> p ht o", o=512), WvT_r[:, :, c0:c0 + 512])
            for tt in range(16):
                for ht in range(16):
                    p.mm(pqk[:], xnT[:, ht * 2048 + tt * 128: ht * 2048 + (tt + 1) * 128],
                         wvl[:, ht * 512:(ht + 1) * 512],
                         start=(ht == 0), stop=(ht == 15))
                p.copy("dve", v_sb[:, tt * 512:(tt + 1) * 512], pqk[:])
            for hh in range(4):
                head = grp * 4 + hh
                for tc in range(4):
                    nsc = 4 * tc + 4
                    for sc in range(nsc):
                        p4 = p4all[:, sc * 512:(sc + 1) * 512]
                        p.mm(pqk[:], kTr[:, hh * 2048 + sc * 128: hh * 2048 + (sc + 1) * 128],
                             qTr[:, hh * 2048 + tc * 512: hh * 2048 + (tc + 1) * 512])
                        j = sc - 4 * tc
                        lo = max(0, j * 128)
                        p.act(p2[:, lo:512], pqk[:, lo:512], AF.Square)
                        p.act(p4[:, lo:512], p2[:, lo:512], AF.Square)
                        if lo > 0:
                            p.memset("dve", p4[:, 0:lo], 0.0)
                        if j >= 0:
                            p.tt("dve", p4[:, lo:lo + 128], p4[:, lo:lo + 128],
                                 mask[:], ALU.mult)
                    for sc in range(nsc):
                        p.mm(poT[:], v_sb[:, sc * 512 + hh * 128: sc * 512 + (hh + 1) * 128],
                             p4all[:, sc * 512:(sc + 1) * 512],
                             start=(sc == 0), stop=(sc == nsc - 1))
                    for sc in range(nsc):
                        p.mm(pden[0:1, :], onesc_bf[:], p4all[:, sc * 512:(sc + 1) * 512],
                             start=(sc == 0), stop=(sc == nsc - 1))
                    p.ts("dve", rden[:], pden[0:1, :], 1e-6, None, ALU.max)
                    p.ew("dve", "reciprocal", rden[:], rden[:],
                         reads=[rden], writes=[rden])
                    p.mm(pbc2[:], ctx.sb["s_c_ones_r"][:], rden[:])
                    p.copy("dve", rdbc[:], pbc2[:])
                    p.tt("dve", oT_f[:], poT[:], rdbc[:], ALU.mult)
                    p.store(attnT_d[b, head * 128:(head + 1) * 128,
                                    tc * 512:(tc + 1) * 512], oT_f[:])
        for hc in range(4):
            p.load(wol[:].rearrange("p (ht o) -> p ht o", o=512),
                   WoT_r[:, :, hc * 512:(hc + 1) * 512])
            for tchunk in range(16):
                for dt in range(16):
                    p.load(v_sb[:, dt * 128:(dt + 1) * 128],
                           attnT_d[b, dt * 128:(dt + 1) * 128,
                                   tchunk * 128:(tchunk + 1) * 128])
                for dt in range(16):
                    p.mm(pqk[:], v_sb[:, dt * 128:(dt + 1) * 128],
                         wol[:, dt * 512:(dt + 1) * 512],
                         start=(dt == 0), stop=(dt == 15))
                p.copy("dve", p2[:], pqk[:])
                p.store(yat_d[b, tchunk * 128:(tchunk + 1) * 128,
                              hc * 512:(hc + 1) * 512], p2[:])
        for tt in range(16):
            r0 = tok0 + tt * 128
            p.load(y_t[:, 0:2048], yat_d[b, tt * 128:(tt + 1) * 128, :])
            ln_tile(ctx, yln[:], y_t[:, 0:2048], D)
            p.load(xres[:], x_in[r0:r0 + 128, :])
            p.tt("dve", yln[:], yln[:], xres[:], ALU.add)
            p.store(x_out[r0:r0 + 128, :], yln[:])


def make_s3_consts():
    T, half = 2048, 64
    theta = 10000.0 ** (-np.arange(half, dtype=np.float64) * 2.0 / 128.0)
    ang = np.arange(T, dtype=np.float64)[None, :] * theta[:, None]
    return {"c_cosT": np.cos(ang).astype(np.float32),
            "c_sinT": np.sin(ang).astype(np.float32)}


def build_s3_program():
    nc, p, ctx = _new_prog()
    N, D = 4096, 2048
    x_in = nc.dram_tensor("x_in", [N, D], F32, kind="ExternalInput").ap()
    Wq = nc.dram_tensor("Wq", [D, D], F32, kind="ExternalInput").ap()
    Wk = nc.dram_tensor("Wk", [D, D], F32, kind="ExternalInput").ap()
    Wv = nc.dram_tensor("Wv", [D, D], F32, kind="ExternalInput").ap()
    Wo = nc.dram_tensor("Wo", [D, D], F32, kind="ExternalInput").ap()
    cosT = nc.dram_tensor("c_cosT", [64, 2048], F32, kind="ExternalInput").ap()
    sinT = nc.dram_tensor("c_sinT", [64, 2048], F32, kind="ExternalInput").ap()
    x_out = nc.dram_tensor("x_out", [N, D], F32, kind="ExternalOutput").ap()
    WqT = nc.dram_tensor("WqT", [D, D], BF16, kind="Internal").ap()
    WkT = nc.dram_tensor("WkT", [D, D], BF16, kind="Internal").ap()
    WvT = nc.dram_tensor("WvT", [D, D], BF16, kind="Internal").ap()
    WoT = nc.dram_tensor("WoT", [D, D], BF16, kind="Internal").ap()
    attnT = nc.dram_tensor("attnT_d", [2, D, 2048], BF16, kind="Internal").ap()
    yat = nc.dram_tensor("yat_d", [2, 2048, 2048], F32, kind="Internal").ap()
    stage_l1attn(ctx, x_in, Wq, Wk, Wv, Wo, cosT, sinT, x_out,
                 WqT, WkT, WvT, WoT, attnT, yat)
    p.barrier()
    return p.build()



def stage_s1a(ctx, x_in, Wqkvz_d, Wba_d, Alog_d, dtb_d,
              mixedT_d, zT_d, G_d, lnb_d):
    p = ctx.p
    D = 2048
    xnT = ctx.S("da_xnT", [128, 16 * 2048], BF16)
    xt = ctx.S("da_xt", [128, 2048])
    xn = ctx.S("da_xn", [128, 2048], BF16)
    wcol = ctx.S("da_wcol", [128, 16 * 128])
    wcol_bf = ctx.S("da_wcolb", [128, 16 * 128], BF16)
    wba = ctx.S("da_wba", [128, 16 * 64])
    wba_bf = ctx.S("da_wbab", [128, 16 * 64], BF16)
    mx_t = ctx.S("da_mxt", [128, 512], BF16)
    baT = ctx.S("da_baT", [64, 2048])
    b_sb = ctx.S("da_b", [32, 2048])
    a_sb = ctx.S("da_a", [32, 2048])
    g_sb = ctx.S("da_g", [32, 2048])
    lnb = ctx.S("da_lnb", [32, 2048])
    G_sb = ctx.S("da_G", [32, 2048])
    z32 = ctx.S("da_z32", [32, 128])
    sp1 = ctx.S("da_sp1", [32, 2048])
    sp2 = ctx.S("da_sp2", [32, 2048])
    sp3 = ctx.S("da_sp3", [32, 2048])
    alog = ctx.S("da_alog", [32, 1])
    dtb = ctx.S("da_dtb", [32, 1])
    eal = ctx.S("da_eal", [32, 1])
    ba_scr = ctx.nc.dram_tensor("ba_scr", [64, 2048], F32, kind="Internal").ap()
    pq = ctx.P("pp_g", [128, 512])
    Wq_r = Wqkvz_d
    p.load(alog[:], Alog_d.rearrange("(a b) -> a b", b=1))
    p.load(dtb[:], dtb_d.rearrange("(a b) -> a b", b=1))
    p.act(eal[:], alog[:], AF.Exp)
    p.memset("dve", z32[:], 0.0)

    for b in range(2):
        tok0 = b * 2048
        for tt in range(16):
            r0 = tok0 + tt * 128
            p.load(xt[:], x_in[r0:r0 + 128, :])
            ln_tile(ctx, xn[:], xt[:], D)
            for ht in range(16):
                transpose128(ctx, xnT[:, ht * 2048 + tt * 128: ht * 2048 + (tt + 1) * 128],
                             xn[:, ht * 128:(ht + 1) * 128])
        for ct in range(96):
            p.load(wcol[:].rearrange("p (ht c) -> p ht c", c=128),
                   Wq_r.rearrange("(ht p) c -> p ht c", p=128)[:, :, ct * 128:(ct + 1) * 128])
            p.copy("pool", wcol_bf[:], wcol[:])
            g6, j = ct // 6, ct % 6
            if j == 0:
                dst, row = mixedT_d, g6 * 128
            elif j == 1:
                dst, row = mixedT_d, 2048 + g6 * 128
            elif j in (2, 3):
                dst, row = mixedT_d, 4096 + (2 * g6 + j - 2) * 128
            else:
                dst, row = zT_d, (2 * g6 + j - 4) * 128
            for tsc in range(4):
                for ht in range(16):
                    p.mm(pq[:], wcol_bf[:, ht * 128:(ht + 1) * 128],
                         xnT[:, ht * 2048 + tsc * 512: ht * 2048 + (tsc + 1) * 512],
                         start=(ht == 0), stop=(ht == 15))
                p.copy("dve", mx_t[:], pq[:])
                p.store(dst[b, row:row + 128, tsc * 512:(tsc + 1) * 512], mx_t[:])
        p.load(wba[:].rearrange("p (ht c) -> p ht c", c=64),
               Wba_d.rearrange("(ht p) c -> p ht c", p=128))
        p.copy("pool", wba_bf[:], wba[:])
        for tsc in range(4):
            for ht in range(16):
                p.mm(pq[0:64, :], wba_bf[:, ht * 64:(ht + 1) * 64],
                     xnT[:, ht * 2048 + tsc * 512: ht * 2048 + (tsc + 1) * 512],
                     start=(ht == 0), stop=(ht == 15))
            p.copy("dve", baT[:, tsc * 512:(tsc + 1) * 512], pq[0:64, :])
        p.store(ba_scr, baT[:])
        for hk in range(16):
            p.load(b_sb[2 * hk:2 * hk + 2, :], ba_scr[4 * hk:4 * hk + 2, :])
            p.load(a_sb[2 * hk:2 * hk + 2, :], ba_scr[4 * hk + 2:4 * hk + 4, :])
        p.ts("dve", sp1[:], a_sb[:], dtb[:, 0:1], None, ALU.add)
        p.act(sp2[:], sp1[:], AF.Abs)
        p.act(sp2[:], sp2[:], AF.Exp, scale=-1.0)
        p.act(sp2[:], sp2[:], AF.Ln, bias=1.0)
        p.act(sp3[:], sp1[:], AF.Relu)
        p.tt("dve", sp1[:], sp2[:], sp3[:], ALU.add)
        p.ts("dve", g_sb[:], sp1[:], eal[:, 0:1], -1.0, ALU.mult, ALU.mult)
        p.act(sp2[:], b_sb[:], AF.Abs)
        p.act(sp2[:], sp2[:], AF.Exp, scale=-1.0)
        p.act(sp2[:], sp2[:], AF.Ln, bias=1.0)
        p.act(sp3[:], b_sb[:], AF.Relu, scale=-1.0)
        p.stt("dve", lnb[:], sp3[:], -1.0, sp2[:], ALU.mult, ALU.subtract)
        for c in range(16):
            p.add("dve", lambda e, c=c: e.tensor_tensor_scan(
                G_sb[:, c * 128:(c + 1) * 128], g_sb[:, c * 128:(c + 1) * 128],
                z32[:], 0.0, ALU.add, ALU.add),
                reads=[g_sb, z32], writes=[G_sb])
        p.store(G_d[b], G_sb[:])
        p.store(lnb_d[b], lnb[:])


def stage_s1b(ctx, x_in, mixedT_d, zT_d, G_d, lnb_d, convw_d, normw_d,
              Wout_d, x_out, gatedT_d, y_d):
    p = ctx.p
    C = 128
    mx = ctx.S("db_mx", [128, 64 * 131], BF16)
    AR = ctx.S("db_AR", [128, 8192])
    cv = ctx.S("db_cv", [128, 8192], BF16)
    Ssb = ctx.S("db_S", [128, 32 * 128])
    khf = ctx.S("db_khf", [128, 32 * 128])
    khb = ctx.S("db_khb", [128, 32 * 128], BF16)
    OT = ctx.S("db_OT", [128, 32 * 128])
    zld = None
    cw = ctx.S("db_cw", [128, 64 * 4], BF16)
    cwf = ctx.S("db_cwf", [128, 64 * 4])
    Gsb = ctx.S("db_G", [32, 2048])
    lnbs = ctx.S("db_lnb", [32, 2048])
    GLB = ctx.S("db_GLB", [32, 2048])
    glbrow = ctx.S("db_glbrow", [1, 128])
    grow = ctx.S("db_grow", [1, 128])
    glb_scr = ctx.nc.dram_tensor("glb_scr", [2, 32, 2048], F32, kind="Internal").ap()
    GT = ctx.S("db_GT", [128, 32])
    lnbT = ctx.S("db_lnbT", [128, 32])
    GLBT = ctx.S("db_GLBT", [128, 32])
    EG = ctx.S("db_EG", [128, 32])
    EGLBn = ctx.S("db_EGLBn", [128, 32])
    bT = ctx.S("db_bT", [128, 32])
    egl_c = ctx.S("db_eglc", [32, 1])
    egl_r = ctx.S("db_eglr", [1, 32])
    scr = ctx.S("db_scr", [128, 2048])
    nr = ctx.S("db_nr", [1, 512])
    rn_sl = ctx.S("db_rn", [1, 512])
    x1 = ctx.S("db_x1", [128, 128])
    xe = ctx.S("db_xe", [128, 128])
    ntf = ctx.S("db_ntf", [128, 128])
    NT0 = ctx.S("db_NT", [128, 128], BF16)
    N0 = ctx.S("db_N0", [128, 128], BF16)
    Xb = ctx.S("db_X", [128, 6 * 128], BF16)
    Pb = ctx.S("db_P", [128, 7 * 128], BF16)
    Vtm = ctx.S("db_Vtm", [128, 128])
    bV = ctx.S("db_bV", [128, 128])
    tmpf = ctx.S("db_tmpf", [128, 128])
    tmpb = ctx.S("db_tmpb", [128, 128], BF16)
    O1 = ctx.S("db_O1", [128, 128])
    Osb = ctx.S("db_Osb", [128, 128])
    Ut = ctx.S("db_Ut", [128, 128])
    khtm = ctx.S("db_khtm", [128, 128])
    dqm = ctx.S("db_dqm", [128, 128])
    sfs = ctx.S("db_sfs", [128, 1])
    nws = ctx.S("db_nw", [128, 1])
    ssqr = ctx.S("db_ssqr", [1, 512])
    rr = ctx.S("db_rr", [1, 512])
    y_t = ctx.S("db_y", [128, 2048])
    pA = ctx.P("pp_g", [128, 512])
    pB = ctx.P("pp_u", [128, 512])
    pC = ctx.P("pp_d0", [128, 512])
    pD = ctx.P("pp_d1", [128, 512])
    ptr = ctx.P("ptr", [128, 128], BF16)
    ptf = ctx.P("ptf", [128, 128], F32)
    ident = ctx.sb["s_c_ident"]
    identb = ctx.sb["s_c_ident_bf"]
    onesr = ctx.sb["s_c_ones_r"]
    onesc = ctx.sb["s_c_ones_c"]
    m_su_neg = ctx.sb["s_c_mask_su_strict_neg"]
    m_su_inc = ctx.sb["s_c_mask_su_incl"]

    p.load(nws[:], normw_d.rearrange("(a b) -> a b", b=1))
    p.load(cw[:].rearrange("p (ct k) -> p ct k", k=4),
           convw_d.rearrange("(ct p) k -> p ct k", p=128))
    LOG_DKI = float(np.log(1.0 / np.sqrt(128.0)))

    for b in range(2):
        tok0 = b * 2048
        p.load(Gsb[:], G_d[b])
        p.load(lnbs[:], lnb_d[b])
        p.tt("dve", GLB[:], Gsb[:], lnbs[:], ALU.add)
        p.store(glb_scr[b], GLB[:])
        p.memset("dve", Ssb[:], 0.0)
        for c in range(16):
            c0 = c * 128
            mxv = mx[:].rearrange("p (ct t) -> p ct t", t=131)
            if c == 0:
                p.memset("dve", mxv[:, :, 0:3], 0.0)
                p.load(mxv[:, :, 3:131],
                       mixedT_d[b].rearrange("(ct p) t -> p ct t", p=128)[:, :, 0:128])
            else:
                p.load(mxv[:, :, 0:131],
                       mixedT_d[b].rearrange("(ct p) t -> p ct t", p=128)[:, :, c0 - 3:c0 + 128])
            ARv = AR[:].rearrange("p (ct t) -> p ct t", t=128)
            cvv = cv[:].rearrange("p (ct t) -> p ct t", t=128)
            cwv = cw[:].rearrange("p (ct k) -> p ct k", k=4)
            for k in range(4):
                wk = cwv[:, :, k:k + 1].broadcast_to([128, 64, 128])
                if k == 0:
                    p.tt("dve", ARv[:], mxv[:, :, 0:128], wk, ALU.mult)
                else:
                    p.tt("dve", cvv[:], mxv[:, :, k:k + 128], wk, ALU.mult)
                    p.tt("dve", ARv[:], ARv[:], cvv[:], ALU.add)
            p.act(cv[:], AR[:], AF.Silu)
            p.trans(ptf[:, 0:32], Gsb[:, c0:c0 + 128], ident[0:32, 0:32])
            p.copy("dve", GT[:], ptf[:, 0:32])
            p.trans(ptf[:, 0:32], lnbs[:, c0:c0 + 128], ident[0:32, 0:32])
            p.copy("dve", lnbT[:], ptf[:, 0:32])
            p.tt("dve", GLBT[:], GT[:], lnbT[:], ALU.add)
            p.act(EG[:], GT[:], AF.Exp)
            p.act(EGLBn[:], GLBT[:], AF.Exp)
            p.ts("dve", EGLBn[:], EGLBn[:], -1.0, None, ALU.mult)
            p.act(bT[:], lnbT[:], AF.Exp)
            p.act(egl_c[:], Gsb[:, c0 + 127:c0 + 128], AF.Exp)
            p.trans(ptf[0:1, 0:32], egl_c[:], ident[0:32, 0:32])
            p.copy("dve", egl_r[:], ptf[0:1, 0:32])
            for base, bias in ((0, LOG_DKI), (2048, 0.0)):
                p.act(scr[:], cv[:, base:base + 2048], AF.Square)
                for s4 in range(4):
                    p.mm(pC[0:1, :], onesc[:], scr[:, s4 * 512:(s4 + 1) * 512])
                    p.copy("dve", nr[:], pC[0:1, :])
                    p.act(nr[:], nr[:], AF.Ln, bias=1e-6)
                    p.act(rn_sl[:], nr[:], AF.Exp, scale=-0.5, bias=bias)
                    p.mm(pC[:], onesr[:], rn_sl[:])
                    p.tt("dve", khf[:, base + s4 * 512: base + (s4 + 1) * 512],
                         cv[:, base + s4 * 512: base + (s4 + 1) * 512], pC[:], ALU.mult)
            p.copy("pool", khb[:], khf[:])
            for h in range(32):
                kv = h // 2
                qf = khf[:, kv * 128:(kv + 1) * 128]
                qb = khb[:, kv * 128:(kv + 1) * 128]
                kf = khf[:, 2048 + kv * 128: 2048 + (kv + 1) * 128]
                kb = khb[:, 2048 + kv * 128: 2048 + (kv + 1) * 128]
                vt = cv[:, 4096 + h * 128: 4096 + (h + 1) * 128]
                Sh = Ssb[:, h * 128:(h + 1) * 128]
                p.mm(pA[:, 0:128], kb, kb)
                p.load(glbrow[:], glb_scr[b, h:h + 1, c0:c0 + 128])
                p.mm(pA[:, 128:256], onesr[:], glbrow[:])
                p.ts("dve", x1[:], pA[:, 128:256], GT[:, h:h + 1], 0.0,
                     ALU.subtract, ALU.min)
                p.act(xe[:], x1[:], AF.Exp)
                p.tt("dve", ntf[:], xe[:], pA[:, 0:128], ALU.mult)
                p.tt("dve", NT0[:], ntf[:], m_su_neg[:], ALU.mult)
                p.trans(ptr[:], NT0[:], identb[:])
                p.copy("dve", N0[:], ptr[:])
                Xs = [N0[:]] ; Ps = [NT0[:]]
                for i in range(6):
                    Pi, Xi = Ps[i], Xs[i]
                    p.mm(pB[:, 0:128], Xi, Pi)
                    pdst = Pb[:, i * 128:(i + 1) * 128]
                    p.copy("dve", pdst, pB[:, 0:128])
                    Ps.append(pdst)
                    if i < 5:
                        p.mm(pB[:, 128:256], Pi, Xi)
                        xdst = Xb[:, i * 128:(i + 1) * 128]
                        p.copy("dve", xdst, pB[:, 128:256])
                        Xs.append(xdst)
                p.trans(ptr[:], vt, identb[:])
                p.copy("dve", Vtm[:], ptr[:])
                p.ts("dve", bV[:], Vtm[:], bT[:, h:h + 1], None, ALU.mult)
                p.mm(pB[:, 128:256], kf, Sh)
                p.stt("dve", tmpf[:], pB[:, 128:256], EGLBn[:, h:h + 1], bV[:],
                      ALU.mult, ALU.add)
                p.copy("pool", tmpb[:], tmpf[:])
                for i in range(6, -1, -1):
                    p.mm(pB[:, 0:128], Ps[i], tmpb[:])
                    p.tt("dve", tmpf[:], pB[:, 0:128], tmpf[:], ALU.add)
                    p.copy("pool", tmpb[:], tmpf[:])
                p.mm(pB[:, 256:384], qf, Sh)
                p.ts("dve", O1[:], pB[:, 256:384], EG[:, h:h + 1], None, ALU.mult)
                p.mm(pA[:, 256:384], kb, qb)
                p.load(grow[:], G_d[b, h:h + 1, c0:c0 + 128])
                p.mm(pA[:, 384:512], onesr[:], grow[:])
                p.ts("dve", x1[:], pA[:, 384:512], GT[:, h:h + 1], 0.0,
                     ALU.subtract, ALU.min)
                p.act(xe[:], x1[:], AF.Exp)
                p.tt("dve", dqm[:], xe[:], m_su_inc[:], ALU.mult)
                p.tt("dve", ntf[:], dqm[:], pA[:, 256:384], ALU.mult)
                p.copy("pool", NT0[:], ntf[:])
                p.mm(pB[:, 0:128], NT0[:], tmpb[:])
                p.tt("dve", Osb[:], pB[:, 0:128], O1[:], ALU.add)
                p.ts("dve", Ut[:], tmpf[:], dqm[:, 127:128], None, ALU.mult)
                p.trans(ptf[:], kf, ident[:])
                p.copy("dve", khtm[:], ptf[:])
                p.mm(pB[:, 384:512], khtm[:], Ut[:])
                p.mm(pC[:, 0:1], onesr[:], egl_r[:, h:h + 1])
                p.copy("dve", sfs[:], pC[:, 0:1])
                p.stt("dve", Sh, Sh, sfs[:, 0:1], pB[:, 384:512], ALU.mult, ALU.add)
                p.trans(ptf[:], Osb[:], ident[:])
                p.copy("dve", OT[:, h * 128:(h + 1) * 128], ptf[:])
            zldv = mx[:, 4096:8192]
            p.load(zldv.rearrange("p (ct t) -> p ct t", t=128),
                   zT_d[b].rearrange("(ct p) t -> p ct t", p=128)[:, :, c0:c0 + 128])
            zs = AR[:, 0:4096]
            p.act(zs, zldv, AF.Silu)
            p.tt("dve", zs, OT[:], zs, ALU.mult)
            sq = AR[:, 4096:8192]
            gN = cv[:, 0:4096]
            p.act(sq, zs, AF.Square)
            for s8 in range(8):
                p.mm(pC[0:1, :], onesc[:], sq[:, s8 * 512:(s8 + 1) * 512])
                p.copy("dve", ssqr[:], pC[0:1, :])
                p.act(ssqr[:], ssqr[:], AF.Ln, scale=1.0 / 128, bias=1e-6)
                p.act(rr[:], ssqr[:], AF.Exp, scale=-0.5)
                p.mm(pC[:], onesr[:], rr[:])
                p.stt("dve", gN[:, s8 * 512:(s8 + 1) * 512],
                      zs[:, s8 * 512:(s8 + 1) * 512], nws[:, 0:1], pC[:],
                      ALU.mult, ALU.mult)
            p.store(gatedT_d[b].rearrange("(ct p) t -> p ct t", p=128)[:, :, c0:c0 + 128],
                    gN.rearrange("p (ct t) -> p ct t", t=128))
        wbf = AR[:].bitcast(BF16)
        for qh in range(4):
            o0 = qh * 512
            for ct in range(32):
                p.load(scr[:, 0:512], Wout_d[ct * 128:(ct + 1) * 128, o0:o0 + 512])
                p.copy("pool", wbf[:, ct * 512:(ct + 1) * 512], scr[:, 0:512])
            atile = mx[:, 0:4096]
            for tc in range(16):
                p.load(atile.rearrange("p (ct t) -> p ct t", t=128),
                       gatedT_d[b].rearrange("(ct p) t -> p ct t", p=128)[:, :, tc * 128:(tc + 1) * 128])
                for ct in range(32):
                    p.mm(pD[:], atile[:, ct * 128:(ct + 1) * 128],
                         wbf[:, ct * 512:(ct + 1) * 512],
                         start=(ct == 0), stop=(ct == 31))
                p.copy("dve", y_t[:, 0:512], pD[:])
                p.store(y_d[b, tc * 128:(tc + 1) * 128, o0:o0 + 512], y_t[:, 0:512])
        yln = AR[:, 0:2048]
        xres = scr
        for tt in range(16):
            r0 = tok0 + tt * 128
            p.load(y_t[:], y_d[b, tt * 128:(tt + 1) * 128, :])
            ln_tile(ctx, yln, y_t[:], 2048)
            p.load(xres[:], x_in[r0:r0 + 128, :])
            p.tt("dve", yln, yln, xres[:], ALU.add)
            p.store(x_out[r0:r0 + 128, :], yln)


def build_s1a_program():
    nc, p, ctx = _new_prog()
    N, D = 4096, 2048
    x_in = nc.dram_tensor("x_in", [N, D], F32, kind="ExternalInput").ap()
    Wqkvz = nc.dram_tensor("W_qkvz", [2048, 12288], F32, kind="ExternalInput").ap()
    Wba = nc.dram_tensor("W_ba", [2048, 64], F32, kind="ExternalInput").ap()
    Alog = nc.dram_tensor("A_log", [32], F32, kind="ExternalInput").ap()
    dtb = nc.dram_tensor("dt_bias", [32], F32, kind="ExternalInput").ap()
    mixedT = nc.dram_tensor("mixedT_d", [2, 8192, 2048], BF16, kind="ExternalOutput").ap()
    zT = nc.dram_tensor("zT_d", [2, 4096, 2048], BF16, kind="ExternalOutput").ap()
    G_d = nc.dram_tensor("G_d", [2, 32, 2048], F32, kind="ExternalOutput").ap()
    lnb_d = nc.dram_tensor("lnb_d", [2, 32, 2048], F32, kind="ExternalOutput").ap()
    stage_s1a(ctx, x_in, Wqkvz, Wba, Alog, dtb, mixedT, zT, G_d, lnb_d)
    p.barrier()
    return p.build()


def build_s1b_program():
    nc, p, ctx = _new_prog()
    N, D = 4096, 2048
    x_in = nc.dram_tensor("x_in", [N, D], F32, kind="ExternalInput").ap()
    mixedT = nc.dram_tensor("mixedT_d", [2, 8192, 2048], BF16, kind="ExternalInput").ap()
    zT = nc.dram_tensor("zT_d", [2, 4096, 2048], BF16, kind="ExternalInput").ap()
    G_d = nc.dram_tensor("G_d", [2, 32, 2048], F32, kind="ExternalInput").ap()
    lnb_d = nc.dram_tensor("lnb_d", [2, 32, 2048], F32, kind="ExternalInput").ap()
    convw = nc.dram_tensor("conv_w", [8192, 4], F32, kind="ExternalInput").ap()
    normw = nc.dram_tensor("norm_w", [128], F32, kind="ExternalInput").ap()
    Wout = nc.dram_tensor("W_out", [4096, 2048], F32, kind="ExternalInput").ap()
    x_out = nc.dram_tensor("x_out", [N, D], F32, kind="ExternalOutput").ap()
    gatedT = nc.dram_tensor("gatedT_d", [2, 4096, 2048], BF16, kind="Internal").ap()
    y_d = nc.dram_tensor("y_d", [2, 2048, 2048], F32, kind="Internal").ap()
    stage_s1b(ctx, x_in, mixedT, zT, G_d, lnb_d, convw, normw, Wout,
              x_out, gatedT, y_d)
    p.barrier()
    return p.build()



def kernel_multilaunch(L0_attn_hidden_states, L0_attn_W_qkvz, L0_attn_W_ba, L0_attn_conv_w,
           L0_attn_dt_bias, L0_attn_A_log, L0_attn_norm_w, L0_attn_W_out,
           L0_ffn_gate_up_packed, L0_ffn_gate_up_scales, L0_ffn_down_packed,
           L0_ffn_down_scales, L1_attn_Wq, L1_attn_Wk, L1_attn_Wv, L1_attn_Wo,
           L1_ffn_w_gate, L1_ffn_w_up, L1_ffn_w_down, L1_ffn_s_gate,
           L1_ffn_s_up, L1_ffn_s_down):
    x = np.ascontiguousarray(np.asarray(L0_attn_hidden_states, np.float32))
    B, T, H = x.shape
    consts = {nm: arr for nm, arr in make_consts().items() if not nm.endswith("_bf")}
    s3c = make_s3_consts()
    shards = [np.ascontiguousarray(x[2 * c:2 * c + 2].reshape(2 * T, H))
              for c in range(NCORES)]
    cores = list(range(NCORES))

    def launch(build_fn, per_core_extra):
        nc = build_fn()
        in_maps = []
        for c in cores:
            m = dict(consts)
            m.update(per_core_extra(c))
            in_maps.append(m)
        res = run_bass_kernel_spmd(nc, in_maps, core_ids=cores)
        return res.results

    w1 = {"W_qkvz": np.asarray(L0_attn_W_qkvz, np.float32),
          "W_ba": np.asarray(L0_attn_W_ba, np.float32),
          "A_log": np.asarray(L0_attn_A_log, np.float32),
          "dt_bias": np.asarray(L0_attn_dt_bias, np.float32)}
    r1 = launch(build_s1a_program, lambda c: dict(w1, x_in=shards[c]))
    w2 = {"conv_w": np.asarray(L0_attn_conv_w, np.float32),
          "norm_w": np.asarray(L0_attn_norm_w, np.float32),
          "W_out": np.asarray(L0_attn_W_out, np.float32)}
    r2 = launch(build_s1b_program, lambda c: dict(
        w2, x_in=shards[c], mixedT_d=r1[c]["mixedT_d"], zT_d=r1[c]["zT_d"],
        G_d=r1[c]["G_d"], lnb_d=r1[c]["lnb_d"]))
    x1 = [np.asarray(r2[c]["x_out"]) for c in cores]
    w3 = {"gup": np.asarray(L0_ffn_gate_up_packed, np.int32),
          "gus": np.asarray(L0_ffn_gate_up_scales, np.float16),
          "dnp": np.asarray(L0_ffn_down_packed, np.int32),
          "dns": np.asarray(L0_ffn_down_scales, np.float16)}
    r3 = launch(build_s2_program, lambda c: dict(w3, x_in=x1[c]))
    x2 = [np.asarray(r3[c]["x_out"]) for c in cores]
    w4 = {"Wq": np.asarray(L1_attn_Wq, np.float32),
          "Wk": np.asarray(L1_attn_Wk, np.float32),
          "Wv": np.asarray(L1_attn_Wv, np.float32),
          "Wo": np.asarray(L1_attn_Wo, np.float32)}
    w4.update(s3c)
    r4 = launch(build_s3_program, lambda c: dict(w4, x_in=x2[c]))
    x3 = [np.asarray(r4[c]["x_out"]) for c in cores]
    w5 = {"w_gate": np.asarray(L1_ffn_w_gate, np.int8),
          "w_up": np.asarray(L1_ffn_w_up, np.int8),
          "w_down": np.asarray(L1_ffn_w_down, np.int8),
          "s_gate": np.asarray(L1_ffn_s_gate, np.float32),
          "s_up": np.asarray(L1_ffn_s_up, np.float32),
          "s_down": np.asarray(L1_ffn_s_down, np.float32)}
    r5 = launch(build_s4_program, lambda c: dict(w5, x_in=x3[c]))
    out = np.concatenate([np.asarray(r5[c]["x_out"], np.float32).reshape(2, T, H)
                          for c in cores], 0)
    return out



def build_fused_program():
    nc, p, ctx = _new_prog()
    N, D = 4096, 2048
    E = "ExternalInput"
    x_in = nc.dram_tensor("x_in", [N, D], F32, kind=E).ap()
    Wqkvz = nc.dram_tensor("W_qkvz", [2048, 12288], F32, kind=E).ap()
    Wba = nc.dram_tensor("W_ba", [2048, 64], F32, kind=E).ap()
    Alog = nc.dram_tensor("A_log", [32], F32, kind=E).ap()
    dtb = nc.dram_tensor("dt_bias", [32], F32, kind=E).ap()
    convw = nc.dram_tensor("conv_w", [8192, 4], F32, kind=E).ap()
    normw = nc.dram_tensor("norm_w", [128], F32, kind=E).ap()
    Wout = nc.dram_tensor("W_out", [4096, 2048], F32, kind=E).ap()
    gup = nc.dram_tensor("gup", [11264, 256], I32, kind=E).ap()
    gus = nc.dram_tensor("gus", [16, 11264], F16, kind=E).ap()
    dnp = nc.dram_tensor("dnp", [2048, 704], I32, kind=E).ap()
    dns = nc.dram_tensor("dns", [44, 2048], F16, kind=E).ap()
    Wq = nc.dram_tensor("Wq", [D, D], F32, kind=E).ap()
    Wk = nc.dram_tensor("Wk", [D, D], F32, kind=E).ap()
    Wv = nc.dram_tensor("Wv", [D, D], F32, kind=E).ap()
    Wo = nc.dram_tensor("Wo", [D, D], F32, kind=E).ap()
    cosT = nc.dram_tensor("c_cosT", [64, 2048], F32, kind=E).ap()
    sinT = nc.dram_tensor("c_sinT", [64, 2048], F32, kind=E).ap()
    wg = nc.dram_tensor("w_gate", [D, 8192], I8, kind=E).ap()
    wu = nc.dram_tensor("w_up", [D, 8192], I8, kind=E).ap()
    wd = nc.dram_tensor("w_down", [8192, D], I8, kind=E).ap()
    sg = nc.dram_tensor("s_gate", [8192], F32, kind=E).ap()
    su = nc.dram_tensor("s_up", [8192], F32, kind=E).ap()
    sd = nc.dram_tensor("s_down", [D], F32, kind=E).ap()
    x_out = nc.dram_tensor("x_out", [N, D], F32, kind="ExternalOutput").ap()
    I_ = "Internal"
    mixedT = nc.dram_tensor("mixedT_d", [2, 8192, 2048], BF16, kind=I_).ap()
    zT = nc.dram_tensor("zT_d", [2, 4096, 2048], BF16, kind=I_).ap()
    G_d = nc.dram_tensor("G_d", [2, 32, 2048], F32, kind=I_).ap()
    lnb_d = nc.dram_tensor("lnb_d", [2, 32, 2048], F32, kind=I_).ap()
    gatedT = nc.dram_tensor("gatedT_d", [2, 4096, 2048], BF16, kind=I_).ap()
    y_d = nc.dram_tensor("y_d", [2, 2048, 2048], F32, kind=I_).ap()
    x1_d = nc.dram_tensor("x1_d", [N, D], F32, kind=I_).ap()
    x2_d = nc.dram_tensor("x2_d", [N, D], F32, kind=I_).ap()
    x3_d = nc.dram_tensor("x3_d", [N, D], F32, kind=I_).ap()
    WguT = nc.dram_tensor("WguT", [2048, 11264], BF16, kind=I_).ap()
    WdT = nc.dram_tensor("WdT", [5632, 2048], BF16, kind=I_).ap()
    fused0 = nc.dram_tensor("fused0_d", [N, 5632], BF16, kind=I_).ap()
    WqT = nc.dram_tensor("WqT", [D, D], BF16, kind=I_).ap()
    WkT = nc.dram_tensor("WkT", [D, D], BF16, kind=I_).ap()
    WvT = nc.dram_tensor("WvT", [D, D], BF16, kind=I_).ap()
    WoT = nc.dram_tensor("WoT", [D, D], BF16, kind=I_).ap()
    attnT = nc.dram_tensor("attnT_d", [2, D, 2048], BF16, kind=I_).ap()
    yat = nc.dram_tensor("yat_d", [2, 2048, 2048], F32, kind=I_).ap()
    fused1 = nc.dram_tensor("fused_d", [N, 8192], BF16, kind=I_).ap()

    stage_s1a(ctx, x_in, Wqkvz, Wba, Alog, dtb, mixedT, zT, G_d, lnb_d)
    p.barrier(); ctx.free_stage_sbuf()
    stage_s1b(ctx, x_in, mixedT, zT, G_d, lnb_d, convw, normw, Wout,
              x1_d, gatedT, y_d)
    p.barrier(); ctx.free_stage_sbuf()
    stage_l0ffn(ctx, x1_d, gup, gus, dnp, dns, x2_d, WguT, WdT, fused0)
    p.barrier(); ctx.free_stage_sbuf()
    stage_l1attn(ctx, x2_d, Wq, Wk, Wv, Wo, cosT, sinT, x3_d,
                 WqT, WkT, WvT, WoT, attnT, yat)
    p.barrier(); ctx.free_stage_sbuf()
    stage_l1ffn(ctx, x3_d, wg, wu, wd, sg, su, sd, x_out, fused1)
    p.barrier()
    return p.build()


def kernel_fused(**inputs):
    x = np.ascontiguousarray(np.asarray(inputs["L0_attn_hidden_states"], np.float32))
    B, T, H = x.shape
    consts = {nm: arr for nm, arr in make_consts().items() if not nm.endswith("_bf")}
    consts.update(make_s3_consts())
    shards = [np.ascontiguousarray(x[2 * c:2 * c + 2].reshape(2 * T, H))
              for c in range(NCORES)]
    w = {"W_qkvz": np.asarray(inputs["L0_attn_W_qkvz"], np.float32),
         "W_ba": np.asarray(inputs["L0_attn_W_ba"], np.float32),
         "A_log": np.asarray(inputs["L0_attn_A_log"], np.float32),
         "dt_bias": np.asarray(inputs["L0_attn_dt_bias"], np.float32),
         "conv_w": np.asarray(inputs["L0_attn_conv_w"], np.float32),
         "norm_w": np.asarray(inputs["L0_attn_norm_w"], np.float32),
         "W_out": np.asarray(inputs["L0_attn_W_out"], np.float32),
         "gup": np.asarray(inputs["L0_ffn_gate_up_packed"], np.int32),
         "gus": np.asarray(inputs["L0_ffn_gate_up_scales"], np.float16),
         "dnp": np.asarray(inputs["L0_ffn_down_packed"], np.int32),
         "dns": np.asarray(inputs["L0_ffn_down_scales"], np.float16),
         "Wq": np.asarray(inputs["L1_attn_Wq"], np.float32),
         "Wk": np.asarray(inputs["L1_attn_Wk"], np.float32),
         "Wv": np.asarray(inputs["L1_attn_Wv"], np.float32),
         "Wo": np.asarray(inputs["L1_attn_Wo"], np.float32),
         "w_gate": np.asarray(inputs["L1_ffn_w_gate"], np.int8),
         "w_up": np.asarray(inputs["L1_ffn_w_up"], np.int8),
         "w_down": np.asarray(inputs["L1_ffn_w_down"], np.int8),
         "s_gate": np.asarray(inputs["L1_ffn_s_gate"], np.float32),
         "s_up": np.asarray(inputs["L1_ffn_s_up"], np.float32),
         "s_down": np.asarray(inputs["L1_ffn_s_down"], np.float32)}
    w.update(consts)
    nc = build_fused_program()
    in_maps = [dict(w, x_in=shards[c]) for c in range(NCORES)]
    res = run_bass_kernel_spmd(nc, in_maps, core_ids=list(range(NCORES)))
    out = np.concatenate([np.asarray(res.results[c]["x_out"], np.float32)
                          .reshape(2, T, H) for c in range(NCORES)], 0)
    return out


def kernel(**inputs):
    return kernel_fused(**inputs)
